# version 14
# baseline (speedup 1.0000x reference)
import numpy as np
from contextlib import ExitStack

import concourse.bass as bass
import concourse.mybir as mybir
from concourse.bass_utils import run_bass_kernel_spmd

F32 = mybir.dt.float32
BF16 = mybir.dt.bfloat16
AF = mybir.ActivationFunctionType
ALU = mybir.AluOpType
AX = mybir.AxisListType.X

D = 2048
ND = 16
F = 5632
NF = 44
H = 32
T = 2048
WARM = 256
TT = T + WARM
TS = 512
BLOCKS = [(0, 256), (256, 512), (768, 512), (1280, 512), (1792, 512)]
NB = len(BLOCKS)
NWELEM = float(F * D)
NPAIR = NF + ND
PGROUP = 8
RING = 21
GSLICE = F * D // 8 // 128
GW = 512
NGJOB = GSLICE // GW
NGAMMA = 3 * NGJOB
GS = 2

EPS_LN = 1e-5
EPS_Q = 1e-8


def build_nc():
    nc = bass.Bass("TRN2", target_bir_lowering=False, debug=False)

    xt_in = nc.dram_tensor("xt_in", [D, TT], F32, kind="ExternalInput").ap()
    win = nc.dram_tensor("win", [D, D], F32, kind="ExternalInput").ap()
    winT = nc.dram_tensor("winT", [ND, 128, D], F32, kind="ExternalInput").ap()
    wdtT = nc.dram_tensor("wdtT", [D, H], F32, kind="ExternalInput").ap()
    bdt_in = nc.dram_tensor("bdt", [H, 1], F32, kind="ExternalInput").ap()
    lasel = nc.dram_tensor("lasel", [H, D], F32, kind="ExternalInput").ap()
    woutT = nc.dram_tensor("woutT", [D, D], F32, kind="ExternalInput").ap()
    w1t = nc.dram_tensor("w1t", [NF, 128, D], F32, kind="ExternalInput").ap()
    w2t = nc.dram_tensor("w2t", [NF, 128, D], F32, kind="ExternalInput").ap()
    w3t = nc.dram_tensor("w3t", [F, D], F32, kind="ExternalInput").ap()
    gsh = nc.dram_tensor("gsh", [3, 128, GSLICE], F32, kind="ExternalInput").ap()
    outT = nc.dram_tensor("outT", [D, T], F32, kind="ExternalOutput").ap()
    gci_t = nc.dram_tensor("gci", [1, 16], F32)
    gco_t = nc.dram_tensor("gco", [1, 16], F32)
    gci = gci_t.ap()
    gco = gco_t.ap()

    yssp = nc.dram_tensor("yssp", [D, T], BF16, kind="Internal").ap()
    woutbf = nc.dram_tensor("woutbf", [D, D], BF16, kind="Internal").ap()
    hidsp = nc.dram_tensor("hidsp", [F, T], BF16, kind="Internal").ap()
    q3sp = nc.dram_tensor("q3sp", [F, T], BF16, kind="Internal").ap()
    taur = nc.dram_tensor("taur", [1, 8], F32, kind="Internal").ap()

    es = ExitStack()

    def sb(name, shape, dtype):
        return es.enter_context(nc.sbuf_tensor(name, shape, dtype))

    def ps(name, shape):
        return es.enter_context(nc.psum_tensor(name, shape, F32))

    def sem(name):
        return es.enter_context(nc.semaphore(name))

    taub = sb("taub", [128, 8], F32)
    xn_es = ExitStack()
    xn_all = xn_es.enter_context(nc.sbuf_tensor("xn_all", [128, ND * TT], BF16))

    s = {}
    for n in (["cl", "ll", "winl0", "winl1", "wdl", "wtl0", "wtl1", "wcm", "ae", "wce",
               "xl", "sq", "sqf", "smm", "f1", "f2", "f3", "f4",
               "ib", "xn", "dmm", "z1", "z2", "z3", "Lm",
               "xpm", "u", "a", "scf", "sc", "ysp0", "ysp1",
               "gl0", "gl1", "gl2", "gcA", "gcV", "gfz", "gmm", "gz",
               "gpw", "gpd", "ccs", "grd",
               "tw", "tb", "bar1",
               "wl0", "wl1", "q1", "q2", "q3", "x1m", "x2m",
               "sg", "hd", "hsp0", "hsp1", "q3sp0", "q3sp1", "bar2",
               "pf", "p2m", "p2f", "xa0", "xa1", "oa", "od",
               "wbl0", "wbl1", "wbs0", "wbs1", "od0", "od1"]
              + ["pl%d" % i for i in range(RING)]):
        s[n] = sem("s_" + n)

    gjobs = [(i // NGJOB, i % NGJOB) for i in range(NGAMMA)]
    gbound = [list(range(17 * k, min(17 * k + 17, NGAMMA))) for k in range(6)]
    NG_A = sum(1 for i in range(NGAMMA) if i % 2 == 0)
    NG_V = NGAMMA - NG_A

    def gcnt(j):
        a = sum(1 for i in range(j + 1) if i % 2 == 0)
        return a, (j + 1) - a

    cum_ysp_par = [[], []]
    np_ = [0, 0]
    for b in range(NB):
        for e in range(ND):
            if b >= 1:
                np_[e % 2] += 1
            cum_ysp_par[0].append(np_[0])
            cum_ysp_par[1].append(np_[1])

    pairs = [("f", f) for f in range(NF)] + [("d", d) for d in range(ND)]
    groups = [list(range(i, min(i + PGROUP, NPAIR))) for i in range(0, NPAIR, PGROUP)]

    with nc.Block() as block:
        a_es = ExitStack()

        def asb(name, shape, dtype):
            return a_es.enter_context(nc.sbuf_tensor(name, shape, dtype))

        def aps(name, shape):
            return a_es.enter_context(nc.psum_tensor(name, shape, F32))

        winT_bf = asb("winT_bf", [128, ND * D], BF16)
        xT = asb("xT", [128, 2 * ND * TS], BF16)

        def xTb(b):
            p_ = b % 2
            return xT[:, p_ * ND * TS:(p_ + 1) * ND * TS]
        gstr = asb("gstr", [128, GS * GW], F32)
        wdtT_sb = asb("wdtT_sb", [128, ND * H], BF16)
        ASel = asb("ASel", [H, D], BF16)
        WcT_bf = asb("WcT_bf", [128, ND * H], BF16)
        bdt_sb = asb("bdt_sb", [H, 1], F32)
        ones_bf = asb("ones_bf", [128, 1], BF16)
        ones_f = asb("ones_f", [128, 1], F32)
        ones_r = asb("ones_r", [1, 128], F32)
        state = asb("state", [128, ND], F32)
        u_sl = asb("u_sl", [128, 2 * TS], F32)
        a_sl = asb("a_sl", [128, 2 * TS], F32)
        ys_sl = asb("ys_sl", [128, 2 * TS], F32)
        xsq_sl = asb("xsq_sl", [128, 4 * TS], BF16)
        zw = asb("zw", [H, 2 * TS], BF16)
        srow = asb("srow", [1, 5 * TS], F32)
        gacc = asb("gacc", [128, NGAMMA + 4], F32)
        gcol = asb("gcol", [128, 4], F32)
        tau_sb = asb("tau_sb", [1, 8], F32)
        gpart_sb = asb("gpart_sb", [1, 16], F32)
        gred_sb = asb("gred_sb", [1, 16], F32)

        pS = aps("pS", [128, TS])
        pQ = aps("pQ", [128, TS])
        pdt = aps("pdt", [128, TS])
        pL = [aps("pL0", [128, TS]), aps("pL1", [128, TS])]
        pxp = [aps("pxp0", [128, TS]), aps("pxp1", [128, TS])]
        pwc = aps("pwc", [128, TS])
        pSl = [pS, pwc]
        pQl = [pQ, pdt]

        win_slot = [xTb(1)[:, 0:D], xTb(1)[:, D:2 * D]]

        @block.sync
        def _(q):
            gload_i = 0

            def gamma_loads(k):
                nonlocal gload_i
                for i in gbound[k]:
                    m, c = gjobs[i]
                    if i >= GS:
                        q.wait_ge(s["gcA"], i - GS + 1)
                    slot = gstr[:, (i % GS) * GW:(i % GS + 1) * GW]
                    src = gsh[m, :, c * GW:(c + 1) * GW]
                    q.dma_start(out=slot, in_=src).then_inc(s["gl%d" % (i % GS)], 16)
                    gload_i += 1

            q.dma_start(out=bdt_sb[:, :], in_=bdt_in[:, :]).then_inc(s["cl"], 16)
            gamma_loads(0)
            for b in range(NB):
                gamma_loads(b + 1)
            q.wait_ge(s["gcA"], NGAMMA)
            q.nop().then_inc(s["bar1"], 1)

        @block.tensor
        def _(q):
            def pe_stats(b, jlist):
                pS_, pQ_ = pSl[b % 2], pQl[b % 2]
                Tb_ = BLOCKS[b][1]
                for j in jlist:
                    idx = 16 * b + j
                    if j == 0:
                        if b == 0:
                            q.wait_ge(s["gz"], 1)
                        if b >= 1:
                            q.wait_ge(s["f3"], b)
                            q.wait_ge(s["xn"], 16 * b)
                        if b % 2 == 1:
                            q.wait_ge(s["wce"], 1)
                            q.wait_ge(s["z1"], b)
                        q.wait_ge(s["xl"], 16 * 16 * (b + 1))
                    q.matmul(pS_[0:1, 0:Tb_], ones_bf[:, 0:1],
                             xTb(b)[:, j * TS:j * TS + Tb_],
                             start=(j == 0), stop=(j == ND - 1),
                             skip_group_check=True)
                    q.wait_ge(s["sq"], idx + 1)
                    q.matmul(pQ_[0:1, 0:Tb_], ones_bf[:, 0:1],
                             xsq_sl[:, (idx % 4) * TS:(idx % 4) * TS + Tb_],
                             start=(j == 0), stop=(j == ND - 1),
                             skip_group_check=True).then_inc(s["sqf"], 1)


            def pe_ib(b):
                pS_, pQ_ = pSl[b % 2], pQl[b % 2]
                Tb_ = BLOCKS[b][1]
                q.wait_ge(s["f3"], b + 1)
                q.matmul(pS_[:, 0:Tb_], ones_r[0:1, :],
                         srow[0:1, 3 * TS:3 * TS + Tb_],
                         start=True, stop=True)
                q.wait_ge(s["f4"], b + 1)
                q.matmul(pQ_[:, 0:Tb_], ones_r[0:1, :],
                         srow[0:1, 4 * TS:4 * TS + Tb_],
                         start=True, stop=True).then_inc(s["ib"], 1)

            def pe_dt(b):
                c0_, Tb_ = BLOCKS[b]
                q.wait_ge(s["xn"], 16 * (b + 1))
                q.wait_ge(s["wce"], 1)
                if b >= 1:
                    q.wait_ge(s["z1"], b)
                last = None
                for d in range(ND):
                    last = q.matmul(pdt[0:H, 0:Tb_],
                                    WcT_bf[:, d * H:(d + 1) * H],
                                    xn_all[:, d * TT + c0_:d * TT + c0_ + Tb_],
                                    start=(d == 0), stop=(d == ND - 1))
                last.then_inc(s["dmm"], 1)

            pe_stats(0, range(ND))
            pe_ib(0)
            q.wait_ge(s["wdl"], 16 * ND)
            for e in range(ND):
                q.wait_ge(s["winl%d" % (e % 2)], 16 * (e // 2 + 1))
                last = None
                for d in range(ND):
                    last = q.matmul(pwc[:, d * H:(d + 1) * H],
                                    win_slot[e % 2][:, d * 128:(d + 1) * 128],
                                    wdtT_sb[:, e * H:(e + 1) * H],
                                    start=(e == 0 and d == 0),
                                    stop=(e == ND - 1 and d == ND - 1),
                                    skip_group_check=True)
                last.then_inc(s["wcm"], 1)
            pe_dt(0)
            for b, (c0, Tb) in enumerate(BLOCKS):
                for e in range(ND):
                    le = 16 * b + e
                    if e == 0:
                        q.wait_ge(s["z3"], b + 1)
                    if le >= 2:
                        q.wait_ge(s["a"], le - 1)
                    q.matmul(pL[e % 2][:, 0:Tb],
                             ASel[:, e * 128:(e + 1) * 128],
                             zw[:, 0:Tb],
                             start=True, stop=True).then_inc(s["Lm"], 1)
                    if b == 0:
                        q.wait_ge(s["wtl%d" % (e % 2)], 16 * (e // 2 + 1))
                    if le >= 2:
                        q.wait_ge(s["u"], le - 1)
                    last = None
                    for d in range(ND):
                        last = q.matmul(pxp[e % 2][:, 0:Tb],
                                        winT_bf[:, e * D + d * 128:e * D + (d + 1) * 128],
                                        xn_all[:, d * TT + c0:d * TT + c0 + Tb],
                                        start=(d == 0), stop=(d == ND - 1))
                    last.then_inc(s["xpm"], 1)
                    if b == 3 and e == 0:
                        q.wait_ge(s["xn"], 16 * 4)
                        for m in range(3):
                            q.wait_ge(s["gfz"], m + 1)
                            q.matmul(pwc[0:1, m:m + 1], gcol[:, m:m + 1],
                                     ones_f[:, 0:1],
                                     start=(m == 0), stop=(m == 2),
                                     skip_group_check=True).then_inc(s["gmm"], 1)
                    if b + 1 < NB:
                        if 1 <= e <= 8:
                            pe_stats(b + 1, range(2 * (e - 1), 2 * e))
                        elif e == 11:
                            pe_ib(b + 1)
                if b + 1 < NB:
                    pe_dt(b + 1)
            q.nop().then_inc(s["bar1"], 1)

        @block.scalar
        def _(q):
            gc_done = 0

            def gamma_jobs(k):
                if k == 0:
                    q.wait_ge(s["gz"], 1)
                for i in gbound[k]:
                    sl_ = gstr[:, (i % GS) * GW:(i % GS + 1) * GW]
                    q.wait_ge(s["gl%d" % (i % GS)], 16 * (i // GS + 1))
                    q.activation(out=sl_, in_=sl_, func=AF.Abs,
                                 accum_out=gacc[:, i:i + 1]).then_inc(s["gcA"], 1)

            q.wait_ge(s["ll"], 16)
            q.activation(out=ASel[:, :], in_=ASel[:, :], func=AF.Exp).then_inc(s["ae"], 1)
            gamma_jobs(0)

            def act_squares(b, jlist):
                Tb_ = BLOCKS[b][1]
                for j in jlist:
                    idx = 16 * b + j
                    if j == 0:
                        q.wait_ge(s["xl"], 16 * 16 * (b + 1))
                    if idx >= 4:
                        q.wait_ge(s["sqf"], idx - 3)
                    q.activation(out=xsq_sl[:, (idx % 4) * TS:(idx % 4) * TS + Tb_],
                                 in_=xTb(b)[:, j * TS:j * TS + Tb_],
                                 func=AF.Square).then_inc(s["sq"], 1)

            def act_f1(b):
                pS_, pQ_ = pSl[b % 2], pQl[b % 2]
                Tb_ = BLOCKS[b][1]
                q.wait_ge(s["sqf"], 16 * (b + 1))
                q.activation(out=srow[0:1, 0:Tb_], in_=pS_[0:1, 0:Tb_],
                             func=AF.Copy, scale=1.0 / D)
                q.activation(out=srow[0:1, TS:TS + Tb_], in_=pQ_[0:1, 0:Tb_],
                             func=AF.Copy, scale=1.0 / D)
                q.activation(out=srow[0:1, 2 * TS:2 * TS + Tb_], in_=srow[0:1, 0:Tb_],
                             func=AF.Square).then_inc(s["f1"], 1)

            def act_f3(b):
                Tb_ = BLOCKS[b][1]
                q.wait_ge(s["f2"], b + 1)
                q.activation(out=srow[0:1, 4 * TS:4 * TS + Tb_],
                             in_=srow[0:1, 3 * TS:3 * TS + Tb_],
                             func=AF.Ln)
                q.activation(out=srow[0:1, 3 * TS:3 * TS + Tb_],
                             in_=srow[0:1, 4 * TS:4 * TS + Tb_],
                             func=AF.Exp, scale=-0.5).then_inc(s["f3"], 1)

            act_squares(0, range(ND))
            act_f1(0)
            act_f3(0)
            for b, (c0, Tb) in enumerate(BLOCKS):
                q.wait_ge(s["z1"], b + 1)
                q.activation(out=zw[:, TS:TS + Tb], in_=zw[:, 0:Tb], func=AF.Abs)
                q.activation(out=zw[:, TS:TS + Tb], in_=zw[:, TS:TS + Tb],
                             func=AF.Exp, scale=-1.0)
                q.activation(out=zw[:, TS:TS + Tb], in_=zw[:, TS:TS + Tb],
                             func=AF.Ln, bias=1.0).then_inc(s["z2"], 1)
                for e in range(ND):
                    le = 16 * b + e
                    q.wait_ge(s["xpm"], le + 1)
                    if le >= 2:
                        q.wait_ge(s["scf"], le - 1)
                    q.activation(out=u_sl[:, (le % 2) * TS:(le % 2) * TS + Tb],
                                 in_=pxp[e % 2][:, 0:Tb],
                                 func=AF.Copy).then_inc(s["u"], 1)
                    q.wait_ge(s["Lm"], le + 1)
                    q.activation(out=a_sl[:, (le % 2) * TS:(le % 2) * TS + Tb],
                                 in_=pL[e % 2][:, 0:Tb],
                                 func=AF.Exp).then_inc(s["a"], 1)
                    if b + 1 < NB:
                        if e <= 7:
                            act_squares(b + 1, range(2 * e, 2 * e + 2))
                        elif e == 9:
                            act_f1(b + 1)
                        elif e == 10:
                            act_f3(b + 1)
                gamma_jobs(b + 1)
            q.wait_ge(s["gfz"], 4)
            q.dma_start(out=taur[:, :], in_=tau_sb[0:1, :]).then_inc(s["tw"], 16)
            q.wait_ge(s["tw"], 16)
            for col in [0, 1, 2, 4, 5, 6]:
                q.dma_start(out=taub[:, col:col + 1],
                            in_=taur[0:1, col:col + 1].partition_broadcast(128)
                            ).then_inc(s["tb"], 16)
            q.wait_ge(s["tb"], 16 * 6)
            q.nop().then_inc(s["bar1"], 1)

        @block.vector
        def _(q):
            def gamma_jobs(k):
                if k == 0:
                    q.wait_ge(s["gz"], 1)

            q.memset(state[:, :], 0.0)
            q.memset(gpart_sb[:, :], 0.0)
            q.memset(tau_sb[:, :], 0.0)
            q.memset(ones_bf[:, :], 1.0)
            q.memset(ones_f[:, :], 1.0)
            q.memset(ones_r[:, :], 1.0)
            q.memset(gacc[:, :], 0.0).then_inc(s["gz"], 1)
            q.wait_ge(s["ae"], 1)
            q.tensor_scalar_mul(out=ASel[:, :], in0=ASel[:, :], scalar1=-1.0)
            gamma_jobs(0)

            def dve_f2(b):
                Tb_ = BLOCKS[b][1]
                q.wait_ge(s["f1"], b + 1)
                q.scalar_tensor_tensor(out=srow[0:1, 3 * TS:3 * TS + Tb_],
                                       in0=srow[0:1, TS:TS + Tb_],
                                       scalar=EPS_LN,
                                       in1=srow[0:1, 2 * TS:2 * TS + Tb_],
                                       op0=ALU.add,
                                       op1=ALU.subtract).then_inc(s["f2"], 1)

            def dve_f4(b):
                Tb_ = BLOCKS[b][1]
                q.wait_ge(s["f3"], b + 1)
                q.tensor_mul(out=srow[0:1, 4 * TS:4 * TS + Tb_],
                             in0=srow[0:1, 0:Tb_],
                             in1=srow[0:1, 3 * TS:3 * TS + Tb_]).then_inc(s["f4"], 1)

            def dve_xn(b, jlist):
                pS_, pQ_ = pSl[b % 2], pQl[b % 2]
                c0_, Tb_ = BLOCKS[b]
                for j in jlist:
                    if j == 0:
                        q.wait_ge(s["ib"], b + 1)
                    dst = xn_all[:, j * TT + c0_:j * TT + c0_ + Tb_]
                    q.tensor_mul(out=dst, in0=xTb(b)[:, j * TS:j * TS + Tb_],
                                 in1=pS_[:, 0:Tb_])
                    q.tensor_sub(out=dst, in0=dst,
                                 in1=pQ_[:, 0:Tb_]).then_inc(s["xn"], 1)

            def dve_z(b):
                Tb_ = BLOCKS[b][1]
                q.wait_ge(s["dmm"], b + 1)
                if b == 0:
                    q.wait_ge(s["cl"], 16)
                q.tensor_scalar_add(out=zw[:, 0:Tb_], in0=pdt[0:H, 0:Tb_],
                                    scalar1=bdt_sb[:, 0:1]).then_inc(s["z1"], 1)

            def dve_dtv(b):
                Tb_ = BLOCKS[b][1]
                q.wait_ge(s["z2"], b + 1)
                q.scalar_tensor_tensor(out=zw[:, 0:Tb_], in0=zw[:, 0:Tb_],
                                       scalar=0.0, in1=zw[:, TS:TS + Tb_],
                                       op0=ALU.max,
                                       op1=ALU.add).then_inc(s["z3"], 1)

            XN_SPREAD = {12: range(0, 4), 13: range(4, 8),
                         14: range(8, 12), 15: range(12, 16)}

            dve_f2(0)
            dve_f4(0)
            dve_xn(0, range(ND))
            q.wait_ge(s["wcm"], ND)
            q.tensor_copy(out=WcT_bf[:, :], in_=pwc[:, 0:ND * H]).then_inc(s["wce"], 1)
            dve_z(0)
            dve_dtv(0)
            for b, (c0, Tb) in enumerate(BLOCKS):
                for e in range(ND):
                    le = 16 * b + e
                    q.wait_ge(s["u"], le + 1)
                    q.wait_ge(s["a"], le + 1)
                    if le >= 2 and cum_ysp_par[e % 2][le - 2] > 0:
                        q.wait_ge(s["ysp%d" % (e % 2)],
                                  16 * cum_ysp_par[e % 2][le - 2])
                    q.tensor_tensor_scan(
                        out=ys_sl[:, (le % 2) * TS:(le % 2) * TS + Tb],
                        data0=a_sl[:, (le % 2) * TS:(le % 2) * TS + Tb],
                        data1=u_sl[:, (le % 2) * TS:(le % 2) * TS + Tb],
                        initial=state[:, e:e + 1],
                        op0=ALU.mult, op1=ALU.add).then_inc(s["sc"], 1)
                    q.wait_ge(s["sc"], le + 1)
                    q.tensor_copy(out=state[:, e:e + 1],
                                  in_=ys_sl[:, (le % 2) * TS + Tb - 1:(le % 2) * TS + Tb]
                                  ).then_inc(s["scf"], 1)
                    if b + 1 < NB:
                        if e == 9:
                            dve_f2(b + 1)
                        elif e == 10:
                            dve_f4(b + 1)
                        elif e >= 12:
                            dve_xn(b + 1, XN_SPREAD[e])
                if b + 1 < NB:
                    dve_z(b + 1)
                    dve_dtv(b + 1)
                gamma_jobs(b + 1)
                if b == 2:
                    q.wait_ge(s["gcA"], NGAMMA)
                    for m in range(3):
                        q.tensor_reduce(out=gcol[:, m:m + 1],
                                        in_=gacc[:, m * NGJOB:(m + 1) * NGJOB],
                                        axis=AX, op=ALU.add).then_inc(s["gfz"], 1)
                if b == 3:
                    q.wait_ge(s["gmm"], 3)
                    q.tensor_copy(out=gpart_sb[0:1, 0:3],
                                  in_=pwc[0:1, 0:3]).then_inc(s["gpw"], 1)
            q.wait_ge(s["grd"], 16)
            for m in range(3):
                q.tensor_scalar(out=tau_sb[0:1, m:m + 1], in0=gred_sb[0:1, m:m + 1],
                                scalar1=0.5 / NWELEM, scalar2=0.5 * EPS_Q,
                                op0=ALU.mult, op1=ALU.add)
                q.tensor_scalar(out=tau_sb[0:1, 4 + m:5 + m], in0=gred_sb[0:1, m:m + 1],
                                scalar1=-0.5 / NWELEM, scalar2=-0.5 * EPS_Q,
                                op0=ALU.mult, op1=ALU.add)
            q.tensor_copy(out=tau_sb[0:1, 3:4],
                          in_=tau_sb[0:1, 0:1]).then_inc(s["gfz"], 1)
            q.nop().then_inc(s["bar1"], 1)

        @block.gpsimd
        def _(q):
            q.dma_start(out=ASel[:, :], in_=lasel[:, :]).then_inc(s["ll"], 16)
            for e in range(ND):
                q.dma_start(out=wdtT_sb[:, e * H:(e + 1) * H],
                            in_=wdtT[e * 128:(e + 1) * 128, :]).then_inc(s["wdl"], 16)
            for e in range(ND):
                q.dma_start(out=xTb(0)[:, e * TS:e * TS + BLOCKS[0][1]],
                            in_=xt_in[e * 128:(e + 1) * 128, 0:BLOCKS[0][1]]
                            ).then_inc(s["xl"], 16)
            for e in range(ND):
                if e >= 2:
                    q.wait_ge(s["wcm"], e - 1)
                q.dma_start(out=win_slot[e % 2],
                            in_=win[e * 128:(e + 1) * 128, :]
                            ).then_inc(s["winl%d" % (e % 2)], 16)
            for e in range(ND):
                par = e % 2
                if e >= 2:
                    q.wait_ge(s["wtl%d" % par], 16 * (e // 2))
                q.dma_start(out=winT_bf[:, e * D:(e + 1) * D],
                            in_=winT[e, :, :]).then_inc(s["wtl%d" % par], 16)
            def spill(sb_, e_):
                c0_, Tb_ = BLOCKS[sb_]
                le = 16 * sb_ + e_
                q.wait_ge(s["sc"], le + 1)
                q.dma_start(out=yssp[e_ * 128:(e_ + 1) * 128,
                                     c0_ - WARM:c0_ - WARM + Tb_],
                            in_=ys_sl[:, (le % 2) * TS:(le % 2) * TS + Tb_]
                            ).then_inc(s["ysp%d" % (e_ % 2)], 16)

            for b, (c0, Tb) in enumerate(BLOCKS):
                if b == 0:
                    continue
                for j in range(ND):
                    if b == 1 and j == 0:
                        q.wait_ge(s["wcm"], ND)
                    if b >= 2 and j == 0:
                        q.wait_ge(s["sq"], 16 * b)
                    if b >= 2:
                        q.wait_ge(s["xn"], 16 * (b - 2) + j + 1)
                    q.dma_start(out=xTb(b)[:, j * TS:j * TS + Tb],
                                in_=xt_in[j * 128:(j + 1) * 128, c0:c0 + Tb]
                                ).then_inc(s["xl"], 16)
                if b >= 2:
                    for e in range(ND):
                        spill(b - 1, e)
            q.wait_ge(s["gpw"], 1)
            q.dma_start(out=gci[:, :], in_=gpart_sb[0:1, :]).then_inc(s["gpd"], 16)
            q.wait_ge(s["gpd"], 16)
            q.collective_compute("AllReduce", ALU.add,
                                 replica_groups=[list(range(8))],
                                 ins=[gci_t.ap().opt()],
                                 outs=[gco_t.ap().opt()]).then_inc(s["ccs"])
            q.wait_ge(s["ccs"], 1)
            q.dma_start(out=gred_sb[0:1, :], in_=gco[:, :]).then_inc(s["grd"], 16)
            for e in range(ND):
                spill(NB - 1, e)
            q.wait_ge(s["xl"], 16 * 16 * NB)
            q.wait_ge(s["wtl0"], 16 * (ND // 2))
            q.wait_ge(s["wtl1"], 16 * (ND // 2))
            q.wait_ge(s["winl0"], 16 * (ND // 2))
            q.wait_ge(s["winl1"], 16 * (ND // 2))
            q.wait_ge(s["wdl"], 16 * ND)
            q.wait_ge(s["ysp0"], 16 * 32)
            q.wait_ge(s["ysp1"], 16 * 32)
            q.nop().then_inc(s["bar1"], 1)

        a_es.close()

        p1 = ExitStack()

        def p1sb(name, shape, dtype):
            return p1.enter_context(nc.sbuf_tensor(name, shape, dtype))

        ws1 = p1sb("ws1", [128, 2 * D], F32)
        ws2 = p1sb("ws2", [128, 2 * D], F32)
        ws3 = p1sb("ws3", [128, 2 * D], F32)
        q1s = p1sb("q1s", [128, 2 * D], BF16)
        q2s = p1sb("q2s", [128, 2 * D], BF16)
        q3s = p1sb("q3s", [128, 2 * D], BF16)
        m2b = p1sb("m2b", [128, D], F32)
        wb_st = p1sb("wb_st", [128, 2 * D], BF16)
        sig = p1sb("sig", [128, D], F32)
        hid = p1sb("hid", [128, 2 * D], BF16)

        px1 = [p1.enter_context(nc.psum_tensor("px1_%d" % i, [128, TS], F32)) for i in range(4)]
        px2 = [p1.enter_context(nc.psum_tensor("px2_%d" % i, [128, TS], F32)) for i in range(4)]

        @block.sync
        def _(q):
            q.wait_ge(s["bar1"], 5)
            for f in range(NF):
                if f >= 2:
                    q.wait_ge(s["q1"], f - 1)
                    q.wait_ge(s["q2"], f - 1)
                    q.wait_ge(s["q3"], f - 1)
                sl = slice((f % 2) * D, (f % 2 + 1) * D)
                wsem = s["wl%d" % (f % 2)]
                q.dma_start(out=ws1[:, sl], in_=w1t[f, :, :]).then_inc(wsem, 16)
                q.dma_start(out=ws2[:, sl], in_=w2t[f, :, :]).then_inc(wsem, 16)
                q.dma_start(out=ws3[:, sl], in_=w3t[f * 128:(f + 1) * 128, :]).then_inc(wsem, 16)
            q.wait_ge(s["wl0"], 16 * 3 * (NF // 2))
            q.wait_ge(s["wl1"], 16 * 3 * (NF // 2))
            q.nop().then_inc(s["bar2"], 1)

        @block.gpsimd
        def _(q):
            def wout_pair(d):
                par = d % 2
                wsl = slice(par * D, (par + 1) * D)
                if d >= 2:
                    q.wait_ge(s["wbs%d" % par], 16 * (d // 2))
                q.dma_start(out=wb_st[:, wsl],
                            in_=woutT[d * 128:(d + 1) * 128, :]
                            ).then_inc(s["wbl%d" % par], 16)
                q.wait_ge(s["wbl%d" % par], 16 * (d // 2 + 1))
                q.dma_start(out=woutbf[d * 128:(d + 1) * 128, :],
                            in_=wb_st[:, wsl]).then_inc(s["wbs%d" % par], 16)

            q.wait_ge(s["bar1"], 5)
            for f in range(NF):
                sl = slice((f % 2) * D, (f % 2 + 1) * D)
                q.wait_ge(s["q3"], f + 1)
                q.dma_start(out=q3sp[f * 128:(f + 1) * 128, :],
                            in_=q3s[:, sl]).then_inc(s["q3sp%d" % (f % 2)], 16)
                q.wait_ge(s["hd"], 4 * f + 4)
                q.dma_start(out=hidsp[f * 128:(f + 1) * 128, :],
                            in_=hid[:, sl]).then_inc(s["hsp%d" % (f % 2)], 16)
                if f >= 2 and f % 2 == 0 and (f - 2) // 2 < ND:
                    wout_pair((f - 2) // 2)
            q.wait_ge(s["q3sp0"], 16 * (NF // 2))
            q.wait_ge(s["q3sp1"], 16 * (NF // 2))
            q.wait_ge(s["hsp0"], 16 * (NF // 2))
            q.wait_ge(s["hsp1"], 16 * (NF // 2))
            q.wait_ge(s["wbs0"], 16 * (ND // 2))
            q.wait_ge(s["wbs1"], 16 * (ND // 2))
            q.nop().then_inc(s["bar2"], 1)

        @block.vector
        def _(q):
            def quant(g):
                sl = slice((g % 2) * D, (g % 2 + 1) * D)
                q.wait_ge(s["wl%d" % (g % 2)], 16 * 3 * (g // 2 + 1))
                if g >= 2:
                    q.wait_ge(s["x2m"], g - 1)
                for (wsb, qsb_, tcol, qsem) in ((ws1, q1s, 0, "q1"),
                                                (ws2, q2s, 1, "q2"),
                                                (ws3, q3s, 2, "q3")):
                    if qsem == "q3" and g >= 2:
                        q.wait_ge(s["q3sp%d" % (g % 2)], 16 * (g // 2))
                    q.tensor_single_scalar(out=m2b[:, :], in_=wsb[:, sl],
                                           scalar=taub[:, 4 + tcol:5 + tcol],
                                           op=ALU.is_lt)
                    q.scalar_tensor_tensor(out=qsb_[:, sl], in0=wsb[:, sl],
                                           scalar=taub[:, tcol:tcol + 1],
                                           in1=m2b[:, :], op0=ALU.is_gt,
                                           op1=ALU.subtract).then_inc(s[qsem], 1)

            q.wait_ge(s["bar1"], 5)
            quant(0)
            quant(1)
            for f in range(NF):
                if f >= 2:
                    q.wait_ge(s["hsp%d" % (f % 2)], 16 * (f // 2))
                q.wait_ge(s["x2m"], f + 1)
                for tb_ in range(4):
                    q.wait_ge(s["sg"], 4 * f + tb_ + 1)
                    q.tensor_mul(out=hid[:, (f % 2) * D + tb_ * TS:(f % 2) * D + (tb_ + 1) * TS],
                                 in0=sig[:, tb_ * TS:(tb_ + 1) * TS],
                                 in1=px2[tb_][:, :]).then_inc(s["hd"], 1)
                if f + 2 < NF:
                    quant(f + 2)
            q.nop().then_inc(s["bar2"], 1)

        @block.scalar
        def _(q):
            q.wait_ge(s["bar1"], 5)
            for f in range(NF):
                q.wait_ge(s["x1m"], f + 1)
                for tb_ in range(4):
                    if f >= 1:
                        q.wait_ge(s["hd"], 4 * (f - 1) + tb_ + 1)
                    q.activation(out=sig[:, tb_ * TS:(tb_ + 1) * TS],
                                 in_=px1[tb_][:, :],
                                 func=AF.Sigmoid).then_inc(s["sg"], 1)
            q.wait_ge(s["hd"], 4 * NF)
            q.nop().then_inc(s["bar2"], 1)

        @block.tensor
        def _(q):
            q.wait_ge(s["bar1"], 5)
            for f in range(NF):
                sl0 = (f % 2) * D
                q.wait_ge(s["q1"], f + 1)
                if f >= 1:
                    q.wait_ge(s["sg"], 4 * f)
                last = None
                for d in range(ND):
                    for tb_ in range(4):
                        last = q.matmul(px1[tb_][:, :],
                                        q1s[:, sl0 + d * 128:sl0 + (d + 1) * 128],
                                        xn_all[:, d * TT + WARM + tb_ * TS:d * TT + WARM + (tb_ + 1) * TS],
                                        start=(d == 0), stop=(d == ND - 1),
                                        skip_group_check=True)
                last.then_inc(s["x1m"], 1)
                q.wait_ge(s["q2"], f + 1)
                if f >= 1:
                    q.wait_ge(s["hd"], 4 * f)
                last = None
                for d in range(ND):
                    for tb_ in range(4):
                        last = q.matmul(px2[tb_][:, :],
                                        q2s[:, sl0 + d * 128:sl0 + (d + 1) * 128],
                                        xn_all[:, d * TT + WARM + tb_ * TS:d * TT + WARM + (tb_ + 1) * TS],
                                        start=(d == 0), stop=(d == ND - 1),
                                        skip_group_check=True)
                last.then_inc(s["x2m"], 1)
            q.nop().then_inc(s["bar2"], 1)

        p1.close()
        xn_es.close()

        p2 = ExitStack()

        def p2sb(name, shape, dtype):
            return p2.enter_context(nc.sbuf_tensor(name, shape, dtype))

        Lring = p2sb("Lring", [128, RING * 1024], BF16)
        Rring = p2sb("Rring", [128, RING * D], BF16)
        out_acc = p2sb("out_acc", [128, 8 * D], F32)
        xadd = p2sb("xadd", [128, 2 * D], F32)
        pfd = p2sb("pfd", [1, 4], F32)

        pp2 = [p2.enter_context(nc.psum_tensor("pp2_%d" % i, [128, TS], F32)) for i in range(2)]

        @block.sync
        def _(q):
            q.wait_ge(s["bar2"], 5)
            for h in range(2):
                for p, (kind, idx) in enumerate(pairs):
                    gp = h * NPAIR + p
                    if gp >= RING:
                        q.wait_ge(s["pf"], gp - RING + 1)
                    slot = gp % RING
                    psem = s["pl%d" % slot]
                    if kind == "f":
                        q.dma_start(out=Rring[:, slot * D:(slot + 1) * D],
                                    in_=hidsp[idx * 128:(idx + 1) * 128, :]
                                    ).then_inc(psem, 16)
                        q.dma_start(out=Lring[:, slot * 1024:(slot + 1) * 1024],
                                    in_=q3sp[idx * 128:(idx + 1) * 128,
                                             h * 1024:(h + 1) * 1024]
                                    ).then_inc(psem, 16)
                    else:
                        q.dma_start(out=Rring[:, slot * D:(slot + 1) * D],
                                    in_=yssp[idx * 128:(idx + 1) * 128, :]
                                    ).then_inc(psem, 16)
                        q.dma_start(out=Lring[:, slot * 1024:(slot + 1) * 1024],
                                    in_=woutbf[idx * 128:(idx + 1) * 128,
                                               h * 1024:(h + 1) * 1024]
                                    ).then_inc(psem, 16)
                for i in range(8):
                    gi = h * 8 + i
                    if gi >= 2:
                        q.wait_ge(s["od%d" % ((gi - 1) % 2)],
                                  16 * ((gi - 1) // 2 + 1))
                    q.dma_start(out=xadd[:, (gi % 2) * D:(gi % 2 + 1) * D],
                                in_=xt_in[gi * 128:(gi + 1) * 128, WARM:TT]
                                ).then_inc(s["xa%d" % (gi % 2)], 16)

        @block.tensor
        def _(q):
            q.wait_ge(s["bar2"], 5)
            tile = 0
            for h in range(2):
                for g, grp in enumerate(groups):
                    for p in grp:
                        gp = h * NPAIR + p
                        q.wait_ge(s["pl%d" % (gp % RING)],
                                  16 * 2 * (gp // RING + 1))
                    for el in range(8):
                        for tb_ in range(4):
                            if tile >= 2:
                                q.wait_ge(s["p2f"], tile - 1)
                            bank = pp2[tile % 2]
                            last = None
                            for i, p in enumerate(grp):
                                slot = (h * NPAIR + p) % RING
                                last = q.matmul(bank[:, :],
                                                Lring[:, slot * 1024 + el * 128:slot * 1024 + (el + 1) * 128],
                                                Rring[:, slot * D + tb_ * TS:slot * D + (tb_ + 1) * TS],
                                                start=(i == 0), stop=(i == len(grp) - 1))
                            last.then_inc(s["p2m"], 1)
                            tile += 1

        @block.vector
        def _(q):
            q.wait_ge(s["bar2"], 5)
            tile = 0
            for h in range(2):
                for g, grp in enumerate(groups):
                    last_g = (g == len(groups) - 1)
                    for el in range(8):
                        if h == 1 and g == 0:
                            q.wait_ge(s["od%d" % (el % 2)], 16 * (el // 2 + 1))
                        for tb_ in range(4):
                            q.wait_ge(s["p2m"], tile + 1)
                            dst = out_acc[:, el * D + tb_ * TS:el * D + (tb_ + 1) * TS]
                            if g == 0:
                                q.tensor_copy(out=dst, in_=pp2[tile % 2][:, :]).then_inc(s["p2f"], 1)
                            else:
                                q.tensor_add(out=dst, in0=dst,
                                             in1=pp2[tile % 2][:, :]).then_inc(s["p2f"], 1)
                            tile += 1
                        if last_g:
                            gi = h * 8 + el
                            q.wait_ge(s["xa%d" % (gi % 2)], 16 * (gi // 2 + 1))
                            q.tensor_add(out=out_acc[:, el * D:(el + 1) * D],
                                         in0=out_acc[:, el * D:(el + 1) * D],
                                         in1=xadd[:, (gi % 2) * D:(gi % 2 + 1) * D]
                                         ).then_inc(s["oa"], 1)
                    q.memset(pfd[0:1, 0:1], 0.0).then_inc(s["pf"], len(grp))

        @block.scalar
        def _(q):
            q.wait_ge(s["bar2"], 5)
            for h in range(2):
                for i in range(8):
                    gi = h * 8 + i
                    par = gi % 2
                    q.wait_ge(s["oa"], gi + 1)
                    if gi >= 2:
                        q.wait_ge(s["od%d" % par], 16 * (gi // 2))
                    q.dma_start(out=outT[gi * 128:(gi + 1) * 128, :],
                                in_=out_acc[:, i * D:(i + 1) * D]
                                ).then_inc(s["od%d" % par], 16)
            q.wait_ge(s["od0"], 16 * 8)
            q.wait_ge(s["od1"], 16 * 8)

        p2.close()

    es.close()
    return nc


def kernel(L0_attn_x, L0_attn_W_in, L0_attn_W_out, L0_attn_log_A,
           L0_attn_W_dt, L0_attn_b_dt, L0_ffn_w1, L0_ffn_w2, L0_ffn_w3):
    x = np.ascontiguousarray(np.asarray(L0_attn_x, dtype=np.float32))
    W_in = np.ascontiguousarray(np.asarray(L0_attn_W_in, dtype=np.float32))
    W_out = np.asarray(L0_attn_W_out, dtype=np.float32)
    log_A = np.asarray(L0_attn_log_A, dtype=np.float32)
    W_dt = np.asarray(L0_attn_W_dt, dtype=np.float32)
    b_dt = np.asarray(L0_attn_b_dt, dtype=np.float32)
    w1 = np.asarray(L0_ffn_w1, dtype=np.float32)
    w2 = np.asarray(L0_ffn_w2, dtype=np.float32)
    w3 = np.asarray(L0_ffn_w3, dtype=np.float32)

    winT = np.ascontiguousarray(
        W_in.T.reshape(ND, 128, ND, 128).transpose(2, 1, 0, 3).reshape(ND, 128, D))
    wdtT = np.ascontiguousarray(W_dt.T)
    woutT = np.ascontiguousarray(W_out.T)
    bdt = np.ascontiguousarray(b_dt.reshape(H, 1))
    lasel = np.full((H, D), -1e4, np.float32)
    for h in range(H):
        lasel[h, h * 64:(h + 1) * 64] = log_A[h, :]
    w1t = np.ascontiguousarray(
        w1.T.reshape(ND, 128, NF, 128).transpose(2, 1, 0, 3).reshape(NF, 128, D))
    w2t = np.ascontiguousarray(
        w2.T.reshape(ND, 128, NF, 128).transpose(2, 1, 0, 3).reshape(NF, 128, D))
    w3t = np.ascontiguousarray(w3.T)

    gflat = np.stack([w1.reshape(-1), w2.reshape(-1), w3.reshape(-1)])
    PSL = 128 * GSLICE

    nc = build_nc()
    in_maps = []
    for c in range(8):
        bb, t0 = c // 4, (c % 4) * T
        xt_b = x[bb].T
        xt_core = np.zeros((D, TT), np.float32)
        lo = t0 - WARM
        if lo < 0:
            xt_core[:, WARM:] = xt_b[:, t0:t0 + T]
        else:
            xt_core[:, :] = xt_b[:, lo:t0 + T]
        gsh_c = np.ascontiguousarray(
            gflat[:, c * PSL:(c + 1) * PSL].reshape(3, 128, GSLICE))
        in_maps.append({
            "xt_in": np.ascontiguousarray(xt_core),
            "win": W_in, "winT": winT, "wdtT": wdtT, "bdt": bdt,
            "lasel": lasel, "woutT": woutT,
            "w1t": w1t, "w2t": w2t, "w3t": w3t, "gsh": gsh_c,
        })
    res = run_bass_kernel_spmd(nc, in_maps, core_ids=list(range(8)))
    out = np.empty((2, 8192, D), np.float32)
    for c in range(8):
        bb, t0 = c // 4, (c % 4) * T
        out[bb, t0:t0 + T, :] = res.results[c]["outT"].T
    return out


# version 15 (final)
# speedup vs baseline: 1.0085x; 1.0085x over previous
import numpy as np
from contextlib import ExitStack

import concourse.bass as bass
import concourse.mybir as mybir
from concourse.bass_utils import run_bass_kernel_spmd

F32 = mybir.dt.float32
BF16 = mybir.dt.bfloat16
AF = mybir.ActivationFunctionType
ALU = mybir.AluOpType
AX = mybir.AxisListType.X

D = 2048
ND = 16
F = 5632
NF = 44
H = 32
T = 2048
WARM = 256
TT = T + WARM
TS = 512
BLOCKS = [(0, 256), (256, 512), (768, 512), (1280, 512), (1792, 512)]
NB = len(BLOCKS)
NWELEM = float(F * D)
NPAIR = NF + ND
PGROUP = 8
RING = 21
GSLICE = F * D // 8 // 128
GW = 512
NGJOB = GSLICE // GW
NGAMMA = 3 * NGJOB
GS = 2

EPS_LN = 1e-5
EPS_Q = 1e-8


def build_nc():
    nc = bass.Bass("TRN2", target_bir_lowering=False, debug=False)

    xt_in = nc.dram_tensor("xt_in", [D, TT], F32, kind="ExternalInput").ap()
    win = nc.dram_tensor("win", [D, D], F32, kind="ExternalInput").ap()
    winT = nc.dram_tensor("winT", [D, D], F32, kind="ExternalInput").ap()
    wdtT = nc.dram_tensor("wdtT", [D, H], F32, kind="ExternalInput").ap()
    bdt_in = nc.dram_tensor("bdt", [H, 1], F32, kind="ExternalInput").ap()
    lasel = nc.dram_tensor("lasel", [H, D], F32, kind="ExternalInput").ap()
    woutT = nc.dram_tensor("woutT", [D, D], F32, kind="ExternalInput").ap()
    w1t = nc.dram_tensor("w1t", [NF, 128, D], F32, kind="ExternalInput").ap()
    w2t = nc.dram_tensor("w2t", [NF, 128, D], F32, kind="ExternalInput").ap()
    w3t = nc.dram_tensor("w3t", [F, D], F32, kind="ExternalInput").ap()
    gsh = nc.dram_tensor("gsh", [3, 128, GSLICE], F32, kind="ExternalInput").ap()
    outT = nc.dram_tensor("outT", [D, T], F32, kind="ExternalOutput").ap()
    gci_t = nc.dram_tensor("gci", [1, 16], F32)
    gco_t = nc.dram_tensor("gco", [1, 16], F32)
    gci = gci_t.ap()
    gco = gco_t.ap()

    yssp = nc.dram_tensor("yssp", [D, T], BF16, kind="Internal").ap()
    woutbf = nc.dram_tensor("woutbf", [D, D], BF16, kind="Internal").ap()
    hidsp = nc.dram_tensor("hidsp", [F, T], BF16, kind="Internal").ap()
    q3sp = nc.dram_tensor("q3sp", [F, T], BF16, kind="Internal").ap()
    taur = nc.dram_tensor("taur", [1, 8], F32, kind="Internal").ap()

    es = ExitStack()

    def sb(name, shape, dtype):
        return es.enter_context(nc.sbuf_tensor(name, shape, dtype))

    def ps(name, shape):
        return es.enter_context(nc.psum_tensor(name, shape, F32))

    def sem(name):
        return es.enter_context(nc.semaphore(name))

    taub = sb("taub", [128, 8], F32)
    xn_es = ExitStack()
    xn_all = xn_es.enter_context(nc.sbuf_tensor("xn_all", [128, ND * TT], BF16))

    s = {}
    for n in (["cl", "ll", "winl0", "winl1", "wdl", "wtl", "wcm", "ae", "wce",
               "xl", "sq", "sqf", "smm", "f1", "f2", "f3", "f4",
               "ib", "xn", "dmm", "z1", "z2", "z3", "Lm",
               "xpm", "u", "a", "scf", "sc", "ysp0", "ysp1",
               "gl0", "gl1", "gl2", "gcA", "gcV", "gfz", "gmm", "gz",
               "gpw", "gpd", "ccs", "grd",
               "tw", "tb", "bar1",
               "wl0", "wl1", "q1", "q2", "q3", "x1m", "x2m",
               "sg", "hd", "hsp0", "hsp1", "q3sp0", "q3sp1", "bar2",
               "pf", "p2m", "p2f", "xa0", "xa1", "oa", "od",
               "wbl0", "wbl1", "wbs0", "wbs1", "od0", "od1"]
              + ["pl%d" % i for i in range(RING)]):
        s[n] = sem("s_" + n)

    gjobs = [(i // NGJOB, i % NGJOB) for i in range(NGAMMA)]
    gbound = [list(range(17 * k, min(17 * k + 17, NGAMMA))) for k in range(6)]
    NG_A = sum(1 for i in range(NGAMMA) if i % 2 == 0)
    NG_V = NGAMMA - NG_A

    def gcnt(j):
        a = sum(1 for i in range(j + 1) if i % 2 == 0)
        return a, (j + 1) - a

    cum_ysp_par = [[], []]
    np_ = [0, 0]
    for b in range(NB):
        for e in range(ND):
            if b >= 1:
                np_[e % 2] += 1
            cum_ysp_par[0].append(np_[0])
            cum_ysp_par[1].append(np_[1])

    pairs = [("f", f) for f in range(NF)] + [("d", d) for d in range(ND)]
    groups = [list(range(i, min(i + PGROUP, NPAIR))) for i in range(0, NPAIR, PGROUP)]

    with nc.Block() as block:
        a_es = ExitStack()

        def asb(name, shape, dtype):
            return a_es.enter_context(nc.sbuf_tensor(name, shape, dtype))

        def aps(name, shape):
            return a_es.enter_context(nc.psum_tensor(name, shape, F32))

        winT_bf = asb("winT_bf", [128, ND * D], BF16)
        xT = asb("xT", [128, 2 * ND * TS], BF16)

        def xTb(b):
            p_ = b % 2
            return xT[:, p_ * ND * TS:(p_ + 1) * ND * TS]
        gstr = asb("gstr", [128, GS * GW], F32)
        wdtT_sb = asb("wdtT_sb", [128, ND * H], BF16)
        ASel = asb("ASel", [H, D], BF16)
        WcT_bf = asb("WcT_bf", [128, ND * H], BF16)
        bdt_sb = asb("bdt_sb", [H, 1], F32)
        ones_bf = asb("ones_bf", [128, 1], BF16)
        ones_f = asb("ones_f", [128, 1], F32)
        ones_r = asb("ones_r", [1, 128], F32)
        state = asb("state", [128, ND], F32)
        u_sl = asb("u_sl", [128, 2 * TS], F32)
        a_sl = asb("a_sl", [128, 2 * TS], F32)
        ys_sl = asb("ys_sl", [128, 2 * TS], F32)
        xsq_sl = asb("xsq_sl", [128, 4 * TS], BF16)
        zw = asb("zw", [H, 2 * TS], BF16)
        srow = asb("srow", [1, 5 * TS], F32)
        gacc = asb("gacc", [128, NGAMMA + 4], F32)
        gcol = asb("gcol", [128, 4], F32)
        tau_sb = asb("tau_sb", [1, 8], F32)
        gpart_sb = asb("gpart_sb", [1, 16], F32)
        gred_sb = asb("gred_sb", [1, 16], F32)

        pS = aps("pS", [128, TS])
        pQ = aps("pQ", [128, TS])
        pdt = aps("pdt", [128, TS])
        pL = [aps("pL0", [128, TS]), aps("pL1", [128, TS])]
        pxp = [aps("pxp0", [128, TS]), aps("pxp1", [128, TS])]
        pwc = aps("pwc", [128, TS])
        pSl = [pS, pwc]
        pQl = [pQ, pdt]

        win_slot = [xTb(1)[:, 0:D], xTb(1)[:, D:2 * D]]

        @block.sync
        def _(q):
            gload_i = 0

            def gamma_loads(k):
                nonlocal gload_i
                for i in gbound[k]:
                    m, c = gjobs[i]
                    if i >= GS:
                        q.wait_ge(s["gcA"], i - GS + 1)
                    slot = gstr[:, (i % GS) * GW:(i % GS + 1) * GW]
                    src = gsh[m, :, c * GW:(c + 1) * GW]
                    q.dma_start(out=slot, in_=src).then_inc(s["gl%d" % (i % GS)], 16)
                    gload_i += 1

            q.dma_start(out=bdt_sb[:, :], in_=bdt_in[:, :]).then_inc(s["cl"], 16)
            gamma_loads(0)
            for b in range(NB):
                gamma_loads(b + 1)
            q.wait_ge(s["gcA"], NGAMMA)
            q.nop().then_inc(s["bar1"], 1)

        @block.tensor
        def _(q):
            q.wait_ge(s["wdl"], 16 * ND)
            for e in range(ND):
                q.wait_ge(s["winl%d" % (e % 2)], 16 * (e // 2 + 1))
                last = None
                for d in range(ND):
                    last = q.matmul(pwc[:, d * H:(d + 1) * H],
                                    win_slot[e % 2][:, d * 128:(d + 1) * 128],
                                    wdtT_sb[:, e * H:(e + 1) * H],
                                    start=(e == 0 and d == 0),
                                    stop=(e == ND - 1 and d == ND - 1),
                                    skip_group_check=True)
                last.then_inc(s["wcm"], 1)
            def pe_stats(b, jlist):
                pS_, pQ_ = pSl[b % 2], pQl[b % 2]
                Tb_ = BLOCKS[b][1]
                for j in jlist:
                    idx = 16 * b + j
                    if j == 0:
                        if b == 0:
                            q.wait_ge(s["gz"], 1)
                        if b >= 1:
                            q.wait_ge(s["f3"], b)
                            q.wait_ge(s["xn"], 16 * b)
                        if b % 2 == 1:
                            q.wait_ge(s["wce"], 1)
                            q.wait_ge(s["z1"], b)
                        q.wait_ge(s["xl"], 16 * 16 * (b + 1))
                    q.matmul(pS_[0:1, 0:Tb_], ones_bf[:, 0:1],
                             xTb(b)[:, j * TS:j * TS + Tb_],
                             start=(j == 0), stop=(j == ND - 1),
                             skip_group_check=True)
                    q.wait_ge(s["sq"], idx + 1)
                    q.matmul(pQ_[0:1, 0:Tb_], ones_bf[:, 0:1],
                             xsq_sl[:, (idx % 4) * TS:(idx % 4) * TS + Tb_],
                             start=(j == 0), stop=(j == ND - 1),
                             skip_group_check=True).then_inc(s["sqf"], 1)


            def pe_ib(b):
                pS_, pQ_ = pSl[b % 2], pQl[b % 2]
                Tb_ = BLOCKS[b][1]
                q.wait_ge(s["f3"], b + 1)
                q.matmul(pS_[:, 0:Tb_], ones_r[0:1, :],
                         srow[0:1, 3 * TS:3 * TS + Tb_],
                         start=True, stop=True)
                q.wait_ge(s["f4"], b + 1)
                q.matmul(pQ_[:, 0:Tb_], ones_r[0:1, :],
                         srow[0:1, 4 * TS:4 * TS + Tb_],
                         start=True, stop=True).then_inc(s["ib"], 1)

            def pe_dt(b):
                c0_, Tb_ = BLOCKS[b]
                q.wait_ge(s["xn"], 16 * (b + 1))
                q.wait_ge(s["wce"], 1)
                if b >= 1:
                    q.wait_ge(s["z1"], b)
                last = None
                for d in range(ND):
                    last = q.matmul(pdt[0:H, 0:Tb_],
                                    WcT_bf[:, d * H:(d + 1) * H],
                                    xn_all[:, d * TT + c0_:d * TT + c0_ + Tb_],
                                    start=(d == 0), stop=(d == ND - 1))
                last.then_inc(s["dmm"], 1)

            pe_stats(0, range(ND))
            pe_ib(0)
            pe_dt(0)
            for b, (c0, Tb) in enumerate(BLOCKS):
                for e in range(ND):
                    le = 16 * b + e
                    if e == 0:
                        q.wait_ge(s["z3"], b + 1)
                    if le >= 2:
                        q.wait_ge(s["a"], le - 1)
                    q.matmul(pL[e % 2][:, 0:Tb],
                             ASel[:, e * 128:(e + 1) * 128],
                             zw[:, 0:Tb],
                             start=True, stop=True).then_inc(s["Lm"], 1)
                    if le == 0:
                        q.wait_ge(s["wtl"], 16 * ND)
                    if le >= 2:
                        q.wait_ge(s["u"], le - 1)
                    last = None
                    for d in range(ND):
                        last = q.matmul(pxp[e % 2][:, 0:Tb],
                                        winT_bf[:, d * D + e * 128:d * D + (e + 1) * 128],
                                        xn_all[:, d * TT + c0:d * TT + c0 + Tb],
                                        start=(d == 0), stop=(d == ND - 1))
                    last.then_inc(s["xpm"], 1)
                    if b == 3 and e == 0:
                        q.wait_ge(s["xn"], 16 * 4)
                        for m in range(3):
                            q.wait_ge(s["gfz"], m + 1)
                            q.matmul(pwc[0:1, m:m + 1], gcol[:, m:m + 1],
                                     ones_f[:, 0:1],
                                     start=(m == 0), stop=(m == 2),
                                     skip_group_check=True).then_inc(s["gmm"], 1)
                    if b + 1 < NB:
                        if 1 <= e <= 8:
                            pe_stats(b + 1, range(2 * (e - 1), 2 * e))
                        elif e == 11:
                            pe_ib(b + 1)
                if b + 1 < NB:
                    pe_dt(b + 1)
            q.nop().then_inc(s["bar1"], 1)

        @block.scalar
        def _(q):
            gc_done = 0

            def gamma_jobs(k):
                if k == 0:
                    q.wait_ge(s["gz"], 1)
                for i in gbound[k]:
                    sl_ = gstr[:, (i % GS) * GW:(i % GS + 1) * GW]
                    q.wait_ge(s["gl%d" % (i % GS)], 16 * (i // GS + 1))
                    q.activation(out=sl_, in_=sl_, func=AF.Abs,
                                 accum_out=gacc[:, i:i + 1]).then_inc(s["gcA"], 1)

            q.wait_ge(s["ll"], 16)
            q.activation(out=ASel[:, :], in_=ASel[:, :], func=AF.Exp).then_inc(s["ae"], 1)
            gamma_jobs(0)

            def act_squares(b, jlist):
                Tb_ = BLOCKS[b][1]
                for j in jlist:
                    idx = 16 * b + j
                    if j == 0:
                        q.wait_ge(s["xl"], 16 * 16 * (b + 1))
                    if idx >= 4:
                        q.wait_ge(s["sqf"], idx - 3)
                    q.activation(out=xsq_sl[:, (idx % 4) * TS:(idx % 4) * TS + Tb_],
                                 in_=xTb(b)[:, j * TS:j * TS + Tb_],
                                 func=AF.Square).then_inc(s["sq"], 1)

            def act_f1(b):
                pS_, pQ_ = pSl[b % 2], pQl[b % 2]
                Tb_ = BLOCKS[b][1]
                q.wait_ge(s["sqf"], 16 * (b + 1))
                q.activation(out=srow[0:1, 0:Tb_], in_=pS_[0:1, 0:Tb_],
                             func=AF.Copy, scale=1.0 / D)
                q.activation(out=srow[0:1, TS:TS + Tb_], in_=pQ_[0:1, 0:Tb_],
                             func=AF.Copy, scale=1.0 / D)
                q.activation(out=srow[0:1, 2 * TS:2 * TS + Tb_], in_=srow[0:1, 0:Tb_],
                             func=AF.Square).then_inc(s["f1"], 1)

            def act_f3(b):
                Tb_ = BLOCKS[b][1]
                q.wait_ge(s["f2"], b + 1)
                q.activation(out=srow[0:1, 4 * TS:4 * TS + Tb_],
                             in_=srow[0:1, 3 * TS:3 * TS + Tb_],
                             func=AF.Ln)
                q.activation(out=srow[0:1, 3 * TS:3 * TS + Tb_],
                             in_=srow[0:1, 4 * TS:4 * TS + Tb_],
                             func=AF.Exp, scale=-0.5).then_inc(s["f3"], 1)

            act_squares(0, range(ND))
            act_f1(0)
            act_f3(0)
            for b, (c0, Tb) in enumerate(BLOCKS):
                q.wait_ge(s["z1"], b + 1)
                q.activation(out=zw[:, TS:TS + Tb], in_=zw[:, 0:Tb], func=AF.Abs)
                q.activation(out=zw[:, TS:TS + Tb], in_=zw[:, TS:TS + Tb],
                             func=AF.Exp, scale=-1.0)
                q.activation(out=zw[:, TS:TS + Tb], in_=zw[:, TS:TS + Tb],
                             func=AF.Ln, bias=1.0).then_inc(s["z2"], 1)
                for e in range(ND):
                    le = 16 * b + e
                    q.wait_ge(s["xpm"], le + 1)
                    if le >= 2:
                        q.wait_ge(s["scf"], le - 1)
                    q.activation(out=u_sl[:, (le % 2) * TS:(le % 2) * TS + Tb],
                                 in_=pxp[e % 2][:, 0:Tb],
                                 func=AF.Copy).then_inc(s["u"], 1)
                    q.wait_ge(s["Lm"], le + 1)
                    q.activation(out=a_sl[:, (le % 2) * TS:(le % 2) * TS + Tb],
                                 in_=pL[e % 2][:, 0:Tb],
                                 func=AF.Exp).then_inc(s["a"], 1)
                    if b + 1 < NB:
                        if e <= 7:
                            act_squares(b + 1, range(2 * e, 2 * e + 2))
                        elif e == 9:
                            act_f1(b + 1)
                        elif e == 10:
                            act_f3(b + 1)
                gamma_jobs(b + 1)
            q.wait_ge(s["gfz"], 4)
            q.dma_start(out=taur[:, :], in_=tau_sb[0:1, :]).then_inc(s["tw"], 16)
            q.wait_ge(s["tw"], 16)
            for col in [0, 1, 2, 4, 5, 6]:
                q.dma_start(out=taub[:, col:col + 1],
                            in_=taur[0:1, col:col + 1].partition_broadcast(128)
                            ).then_inc(s["tb"], 16)
            q.wait_ge(s["tb"], 16 * 6)
            q.nop().then_inc(s["bar1"], 1)

        @block.vector
        def _(q):
            def gamma_jobs(k):
                if k == 0:
                    q.wait_ge(s["gz"], 1)

            q.memset(state[:, :], 0.0)
            q.memset(gpart_sb[:, :], 0.0)
            q.memset(tau_sb[:, :], 0.0)
            q.memset(ones_bf[:, :], 1.0)
            q.memset(ones_f[:, :], 1.0)
            q.memset(ones_r[:, :], 1.0)
            q.memset(gacc[:, :], 0.0).then_inc(s["gz"], 1)
            q.wait_ge(s["ae"], 1)
            q.tensor_scalar_mul(out=ASel[:, :], in0=ASel[:, :], scalar1=-1.0)
            gamma_jobs(0)

            def dve_f2(b):
                Tb_ = BLOCKS[b][1]
                q.wait_ge(s["f1"], b + 1)
                q.scalar_tensor_tensor(out=srow[0:1, 3 * TS:3 * TS + Tb_],
                                       in0=srow[0:1, TS:TS + Tb_],
                                       scalar=EPS_LN,
                                       in1=srow[0:1, 2 * TS:2 * TS + Tb_],
                                       op0=ALU.add,
                                       op1=ALU.subtract).then_inc(s["f2"], 1)

            def dve_f4(b):
                Tb_ = BLOCKS[b][1]
                q.wait_ge(s["f3"], b + 1)
                q.tensor_mul(out=srow[0:1, 4 * TS:4 * TS + Tb_],
                             in0=srow[0:1, 0:Tb_],
                             in1=srow[0:1, 3 * TS:3 * TS + Tb_]).then_inc(s["f4"], 1)

            def dve_xn(b, jlist):
                pS_, pQ_ = pSl[b % 2], pQl[b % 2]
                c0_, Tb_ = BLOCKS[b]
                for j in jlist:
                    if j == 0:
                        q.wait_ge(s["ib"], b + 1)
                    dst = xn_all[:, j * TT + c0_:j * TT + c0_ + Tb_]
                    q.tensor_mul(out=dst, in0=xTb(b)[:, j * TS:j * TS + Tb_],
                                 in1=pS_[:, 0:Tb_])
                    q.tensor_sub(out=dst, in0=dst,
                                 in1=pQ_[:, 0:Tb_]).then_inc(s["xn"], 1)

            def dve_z(b):
                Tb_ = BLOCKS[b][1]
                q.wait_ge(s["dmm"], b + 1)
                if b == 0:
                    q.wait_ge(s["cl"], 16)
                q.tensor_scalar_add(out=zw[:, 0:Tb_], in0=pdt[0:H, 0:Tb_],
                                    scalar1=bdt_sb[:, 0:1]).then_inc(s["z1"], 1)

            def dve_dtv(b):
                Tb_ = BLOCKS[b][1]
                q.wait_ge(s["z2"], b + 1)
                q.scalar_tensor_tensor(out=zw[:, 0:Tb_], in0=zw[:, 0:Tb_],
                                       scalar=0.0, in1=zw[:, TS:TS + Tb_],
                                       op0=ALU.max,
                                       op1=ALU.add).then_inc(s["z3"], 1)

            XN_SPREAD = {12: range(0, 4), 13: range(4, 8),
                         14: range(8, 12), 15: range(12, 16)}

            dve_f2(0)
            dve_f4(0)
            dve_xn(0, range(ND))
            q.wait_ge(s["wcm"], ND)
            q.tensor_copy(out=WcT_bf[:, :], in_=pwc[:, 0:ND * H]).then_inc(s["wce"], 1)
            dve_z(0)
            dve_dtv(0)
            for b, (c0, Tb) in enumerate(BLOCKS):
                for e in range(ND):
                    le = 16 * b + e
                    q.wait_ge(s["u"], le + 1)
                    q.wait_ge(s["a"], le + 1)
                    if le >= 2 and cum_ysp_par[e % 2][le - 2] > 0:
                        q.wait_ge(s["ysp%d" % (e % 2)],
                                  16 * cum_ysp_par[e % 2][le - 2])
                    q.tensor_tensor_scan(
                        out=ys_sl[:, (le % 2) * TS:(le % 2) * TS + Tb],
                        data0=a_sl[:, (le % 2) * TS:(le % 2) * TS + Tb],
                        data1=u_sl[:, (le % 2) * TS:(le % 2) * TS + Tb],
                        initial=state[:, e:e + 1],
                        op0=ALU.mult, op1=ALU.add).then_inc(s["sc"], 1)
                    q.wait_ge(s["sc"], le + 1)
                    q.tensor_copy(out=state[:, e:e + 1],
                                  in_=ys_sl[:, (le % 2) * TS + Tb - 1:(le % 2) * TS + Tb]
                                  ).then_inc(s["scf"], 1)
                    if b + 1 < NB:
                        if e == 9:
                            dve_f2(b + 1)
                        elif e == 10:
                            dve_f4(b + 1)
                        elif e >= 12:
                            dve_xn(b + 1, XN_SPREAD[e])
                if b + 1 < NB:
                    dve_z(b + 1)
                    dve_dtv(b + 1)
                gamma_jobs(b + 1)
                if b == 2:
                    q.wait_ge(s["gcA"], NGAMMA)
                    for m in range(3):
                        q.tensor_reduce(out=gcol[:, m:m + 1],
                                        in_=gacc[:, m * NGJOB:(m + 1) * NGJOB],
                                        axis=AX, op=ALU.add).then_inc(s["gfz"], 1)
                if b == 3:
                    q.wait_ge(s["gmm"], 3)
                    q.tensor_copy(out=gpart_sb[0:1, 0:3],
                                  in_=pwc[0:1, 0:3]).then_inc(s["gpw"], 1)
            q.wait_ge(s["grd"], 16)
            for m in range(3):
                q.tensor_scalar(out=tau_sb[0:1, m:m + 1], in0=gred_sb[0:1, m:m + 1],
                                scalar1=0.5 / NWELEM, scalar2=0.5 * EPS_Q,
                                op0=ALU.mult, op1=ALU.add)
                q.tensor_scalar(out=tau_sb[0:1, 4 + m:5 + m], in0=gred_sb[0:1, m:m + 1],
                                scalar1=-0.5 / NWELEM, scalar2=-0.5 * EPS_Q,
                                op0=ALU.mult, op1=ALU.add)
            q.tensor_copy(out=tau_sb[0:1, 3:4],
                          in_=tau_sb[0:1, 0:1]).then_inc(s["gfz"], 1)
            q.nop().then_inc(s["bar1"], 1)

        @block.gpsimd
        def _(q):
            q.dma_start(out=ASel[:, :], in_=lasel[:, :]).then_inc(s["ll"], 16)
            for e in range(ND):
                q.dma_start(out=wdtT_sb[:, e * H:(e + 1) * H],
                            in_=wdtT[e * 128:(e + 1) * 128, :]).then_inc(s["wdl"], 16)
            for e in range(ND):
                q.dma_start(out=xTb(0)[:, e * TS:e * TS + BLOCKS[0][1]],
                            in_=xt_in[e * 128:(e + 1) * 128, 0:BLOCKS[0][1]]
                            ).then_inc(s["xl"], 16)
            for e in range(ND):
                if e >= 2:
                    q.wait_ge(s["wcm"], e - 1)
                q.dma_start(out=win_slot[e % 2],
                            in_=win[e * 128:(e + 1) * 128, :]
                            ).then_inc(s["winl%d" % (e % 2)], 16)
            for d in range(ND):
                q.dma_start(out=winT_bf[:, d * D:(d + 1) * D],
                            in_=winT[d * 128:(d + 1) * 128, :]).then_inc(s["wtl"], 16)
            def spill(sb_, e_):
                c0_, Tb_ = BLOCKS[sb_]
                le = 16 * sb_ + e_
                q.wait_ge(s["sc"], le + 1)
                q.dma_start(out=yssp[e_ * 128:(e_ + 1) * 128,
                                     c0_ - WARM:c0_ - WARM + Tb_],
                            in_=ys_sl[:, (le % 2) * TS:(le % 2) * TS + Tb_]
                            ).then_inc(s["ysp%d" % (e_ % 2)], 16)

            for b, (c0, Tb) in enumerate(BLOCKS):
                if b == 0:
                    continue
                for j in range(ND):
                    if b == 1 and j == 0:
                        q.wait_ge(s["wcm"], ND)
                    if b >= 2 and j == 0:
                        q.wait_ge(s["sq"], 16 * b)
                    if b >= 2:
                        q.wait_ge(s["xn"], 16 * (b - 2) + j + 1)
                    q.dma_start(out=xTb(b)[:, j * TS:j * TS + Tb],
                                in_=xt_in[j * 128:(j + 1) * 128, c0:c0 + Tb]
                                ).then_inc(s["xl"], 16)
                if b >= 2:
                    for e in range(ND):
                        spill(b - 1, e)
            q.wait_ge(s["gpw"], 1)
            q.dma_start(out=gci[:, :], in_=gpart_sb[0:1, :]).then_inc(s["gpd"], 16)
            q.wait_ge(s["gpd"], 16)
            q.collective_compute("AllReduce", ALU.add,
                                 replica_groups=[list(range(8))],
                                 ins=[gci_t.ap().opt()],
                                 outs=[gco_t.ap().opt()]).then_inc(s["ccs"])
            q.wait_ge(s["ccs"], 1)
            q.dma_start(out=gred_sb[0:1, :], in_=gco[:, :]).then_inc(s["grd"], 16)
            for e in range(ND):
                spill(NB - 1, e)
            q.wait_ge(s["xl"], 16 * 16 * NB)
            q.wait_ge(s["wtl"], 16 * ND)
            q.wait_ge(s["winl0"], 16 * (ND // 2))
            q.wait_ge(s["winl1"], 16 * (ND // 2))
            q.wait_ge(s["wdl"], 16 * ND)
            q.wait_ge(s["ysp0"], 16 * 32)
            q.wait_ge(s["ysp1"], 16 * 32)
            q.nop().then_inc(s["bar1"], 1)

        a_es.close()

        p1 = ExitStack()

        def p1sb(name, shape, dtype):
            return p1.enter_context(nc.sbuf_tensor(name, shape, dtype))

        ws1 = p1sb("ws1", [128, 2 * D], F32)
        ws2 = p1sb("ws2", [128, 2 * D], F32)
        ws3 = p1sb("ws3", [128, 2 * D], F32)
        q1s = p1sb("q1s", [128, 2 * D], BF16)
        q2s = p1sb("q2s", [128, 2 * D], BF16)
        q3s = p1sb("q3s", [128, 2 * D], BF16)
        m2b = p1sb("m2b", [128, D], F32)
        wb_st = p1sb("wb_st", [128, 2 * D], BF16)
        sig = p1sb("sig", [128, D], F32)
        hid = p1sb("hid", [128, 2 * D], BF16)

        px1 = [p1.enter_context(nc.psum_tensor("px1_%d" % i, [128, TS], F32)) for i in range(4)]
        px2 = [p1.enter_context(nc.psum_tensor("px2_%d" % i, [128, TS], F32)) for i in range(4)]

        @block.sync
        def _(q):
            q.wait_ge(s["bar1"], 5)
            for f in range(NF):
                if f >= 2:
                    q.wait_ge(s["q1"], f - 1)
                    q.wait_ge(s["q2"], f - 1)
                    q.wait_ge(s["q3"], f - 1)
                sl = slice((f % 2) * D, (f % 2 + 1) * D)
                wsem = s["wl%d" % (f % 2)]
                q.dma_start(out=ws1[:, sl], in_=w1t[f, :, :]).then_inc(wsem, 16)
                q.dma_start(out=ws2[:, sl], in_=w2t[f, :, :]).then_inc(wsem, 16)
                q.dma_start(out=ws3[:, sl], in_=w3t[f * 128:(f + 1) * 128, :]).then_inc(wsem, 16)
            q.wait_ge(s["wl0"], 16 * 3 * (NF // 2))
            q.wait_ge(s["wl1"], 16 * 3 * (NF // 2))
            q.nop().then_inc(s["bar2"], 1)

        @block.gpsimd
        def _(q):
            def wout_pair(d):
                par = d % 2
                wsl = slice(par * D, (par + 1) * D)
                if d >= 2:
                    q.wait_ge(s["wbs%d" % par], 16 * (d // 2))
                q.dma_start(out=wb_st[:, wsl],
                            in_=woutT[d * 128:(d + 1) * 128, :]
                            ).then_inc(s["wbl%d" % par], 16)
                q.wait_ge(s["wbl%d" % par], 16 * (d // 2 + 1))
                q.dma_start(out=woutbf[d * 128:(d + 1) * 128, :],
                            in_=wb_st[:, wsl]).then_inc(s["wbs%d" % par], 16)

            q.wait_ge(s["bar1"], 5)
            for f in range(NF):
                sl = slice((f % 2) * D, (f % 2 + 1) * D)
                q.wait_ge(s["q3"], f + 1)
                q.dma_start(out=q3sp[f * 128:(f + 1) * 128, :],
                            in_=q3s[:, sl]).then_inc(s["q3sp%d" % (f % 2)], 16)
                q.wait_ge(s["hd"], 4 * f + 4)
                q.dma_start(out=hidsp[f * 128:(f + 1) * 128, :],
                            in_=hid[:, sl]).then_inc(s["hsp%d" % (f % 2)], 16)
                if f >= 2 and f % 2 == 0 and (f - 2) // 2 < ND:
                    wout_pair((f - 2) // 2)
            q.wait_ge(s["q3sp0"], 16 * (NF // 2))
            q.wait_ge(s["q3sp1"], 16 * (NF // 2))
            q.wait_ge(s["hsp0"], 16 * (NF // 2))
            q.wait_ge(s["hsp1"], 16 * (NF // 2))
            q.wait_ge(s["wbs0"], 16 * (ND // 2))
            q.wait_ge(s["wbs1"], 16 * (ND // 2))
            q.nop().then_inc(s["bar2"], 1)

        @block.vector
        def _(q):
            def quant(g):
                sl = slice((g % 2) * D, (g % 2 + 1) * D)
                q.wait_ge(s["wl%d" % (g % 2)], 16 * 3 * (g // 2 + 1))
                if g >= 2:
                    q.wait_ge(s["x2m"], g - 1)
                for (wsb, qsb_, tcol, qsem) in ((ws1, q1s, 0, "q1"),
                                                (ws2, q2s, 1, "q2"),
                                                (ws3, q3s, 2, "q3")):
                    if qsem == "q3" and g >= 2:
                        q.wait_ge(s["q3sp%d" % (g % 2)], 16 * (g // 2))
                    q.tensor_single_scalar(out=m2b[:, :], in_=wsb[:, sl],
                                           scalar=taub[:, 4 + tcol:5 + tcol],
                                           op=ALU.is_lt)
                    q.scalar_tensor_tensor(out=qsb_[:, sl], in0=wsb[:, sl],
                                           scalar=taub[:, tcol:tcol + 1],
                                           in1=m2b[:, :], op0=ALU.is_gt,
                                           op1=ALU.subtract).then_inc(s[qsem], 1)

            q.wait_ge(s["bar1"], 5)
            quant(0)
            quant(1)
            for f in range(NF):
                if f >= 2:
                    q.wait_ge(s["hsp%d" % (f % 2)], 16 * (f // 2))
                q.wait_ge(s["x2m"], f + 1)
                for tb_ in range(4):
                    q.wait_ge(s["sg"], 4 * f + tb_ + 1)
                    q.tensor_mul(out=hid[:, (f % 2) * D + tb_ * TS:(f % 2) * D + (tb_ + 1) * TS],
                                 in0=sig[:, tb_ * TS:(tb_ + 1) * TS],
                                 in1=px2[tb_][:, :]).then_inc(s["hd"], 1)
                if f + 2 < NF:
                    quant(f + 2)
            q.nop().then_inc(s["bar2"], 1)

        @block.scalar
        def _(q):
            q.wait_ge(s["bar1"], 5)
            for f in range(NF):
                q.wait_ge(s["x1m"], f + 1)
                for tb_ in range(4):
                    if f >= 1:
                        q.wait_ge(s["hd"], 4 * (f - 1) + tb_ + 1)
                    q.activation(out=sig[:, tb_ * TS:(tb_ + 1) * TS],
                                 in_=px1[tb_][:, :],
                                 func=AF.Sigmoid).then_inc(s["sg"], 1)
            q.wait_ge(s["hd"], 4 * NF)
            q.nop().then_inc(s["bar2"], 1)

        @block.tensor
        def _(q):
            q.wait_ge(s["bar1"], 5)
            for f in range(NF):
                sl0 = (f % 2) * D
                q.wait_ge(s["q1"], f + 1)
                if f >= 1:
                    q.wait_ge(s["sg"], 4 * f)
                last = None
                for d in range(ND):
                    for tb_ in range(4):
                        last = q.matmul(px1[tb_][:, :],
                                        q1s[:, sl0 + d * 128:sl0 + (d + 1) * 128],
                                        xn_all[:, d * TT + WARM + tb_ * TS:d * TT + WARM + (tb_ + 1) * TS],
                                        start=(d == 0), stop=(d == ND - 1),
                                        skip_group_check=True)
                last.then_inc(s["x1m"], 1)
                q.wait_ge(s["q2"], f + 1)
                if f >= 1:
                    q.wait_ge(s["hd"], 4 * f)
                last = None
                for d in range(ND):
                    for tb_ in range(4):
                        last = q.matmul(px2[tb_][:, :],
                                        q2s[:, sl0 + d * 128:sl0 + (d + 1) * 128],
                                        xn_all[:, d * TT + WARM + tb_ * TS:d * TT + WARM + (tb_ + 1) * TS],
                                        start=(d == 0), stop=(d == ND - 1),
                                        skip_group_check=True)
                last.then_inc(s["x2m"], 1)
            q.nop().then_inc(s["bar2"], 1)

        p1.close()
        xn_es.close()

        p2 = ExitStack()

        def p2sb(name, shape, dtype):
            return p2.enter_context(nc.sbuf_tensor(name, shape, dtype))

        Lring = p2sb("Lring", [128, RING * 1024], BF16)
        Rring = p2sb("Rring", [128, RING * D], BF16)
        out_acc = p2sb("out_acc", [128, 8 * D], F32)
        xadd = p2sb("xadd", [128, 2 * D], F32)
        pfd = p2sb("pfd", [1, 4], F32)

        pp2 = [p2.enter_context(nc.psum_tensor("pp2_%d" % i, [128, TS], F32)) for i in range(2)]

        @block.sync
        def _(q):
            q.wait_ge(s["bar2"], 5)
            for h in range(2):
                for p, (kind, idx) in enumerate(pairs):
                    gp = h * NPAIR + p
                    if gp >= RING:
                        q.wait_ge(s["pf"], gp - RING + 1)
                    slot = gp % RING
                    psem = s["pl%d" % slot]
                    if kind == "f":
                        q.dma_start(out=Rring[:, slot * D:(slot + 1) * D],
                                    in_=hidsp[idx * 128:(idx + 1) * 128, :]
                                    ).then_inc(psem, 16)
                        q.dma_start(out=Lring[:, slot * 1024:(slot + 1) * 1024],
                                    in_=q3sp[idx * 128:(idx + 1) * 128,
                                             h * 1024:(h + 1) * 1024]
                                    ).then_inc(psem, 16)
                    else:
                        q.dma_start(out=Rring[:, slot * D:(slot + 1) * D],
                                    in_=yssp[idx * 128:(idx + 1) * 128, :]
                                    ).then_inc(psem, 16)
                        q.dma_start(out=Lring[:, slot * 1024:(slot + 1) * 1024],
                                    in_=woutbf[idx * 128:(idx + 1) * 128,
                                               h * 1024:(h + 1) * 1024]
                                    ).then_inc(psem, 16)
                for i in range(8):
                    gi = h * 8 + i
                    if gi >= 2:
                        q.wait_ge(s["od%d" % ((gi - 1) % 2)],
                                  16 * ((gi - 1) // 2 + 1))
                    q.dma_start(out=xadd[:, (gi % 2) * D:(gi % 2 + 1) * D],
                                in_=xt_in[gi * 128:(gi + 1) * 128, WARM:TT]
                                ).then_inc(s["xa%d" % (gi % 2)], 16)

        @block.tensor
        def _(q):
            q.wait_ge(s["bar2"], 5)
            tile = 0
            for h in range(2):
                for g, grp in enumerate(groups):
                    for p in grp:
                        gp = h * NPAIR + p
                        q.wait_ge(s["pl%d" % (gp % RING)],
                                  16 * 2 * (gp // RING + 1))
                    for el in range(8):
                        for tb_ in range(4):
                            if tile >= 2:
                                q.wait_ge(s["p2f"], tile - 1)
                            bank = pp2[tile % 2]
                            last = None
                            for i, p in enumerate(grp):
                                slot = (h * NPAIR + p) % RING
                                last = q.matmul(bank[:, :],
                                                Lring[:, slot * 1024 + el * 128:slot * 1024 + (el + 1) * 128],
                                                Rring[:, slot * D + tb_ * TS:slot * D + (tb_ + 1) * TS],
                                                start=(i == 0), stop=(i == len(grp) - 1))
                            last.then_inc(s["p2m"], 1)
                            tile += 1

        @block.vector
        def _(q):
            q.wait_ge(s["bar2"], 5)
            tile = 0
            for h in range(2):
                for g, grp in enumerate(groups):
                    last_g = (g == len(groups) - 1)
                    for el in range(8):
                        if h == 1 and g == 0:
                            q.wait_ge(s["od%d" % (el % 2)], 16 * (el // 2 + 1))
                        for tb_ in range(4):
                            q.wait_ge(s["p2m"], tile + 1)
                            dst = out_acc[:, el * D + tb_ * TS:el * D + (tb_ + 1) * TS]
                            if g == 0:
                                q.tensor_copy(out=dst, in_=pp2[tile % 2][:, :]).then_inc(s["p2f"], 1)
                            else:
                                q.tensor_add(out=dst, in0=dst,
                                             in1=pp2[tile % 2][:, :]).then_inc(s["p2f"], 1)
                            tile += 1
                        if last_g:
                            gi = h * 8 + el
                            q.wait_ge(s["xa%d" % (gi % 2)], 16 * (gi // 2 + 1))
                            q.tensor_add(out=out_acc[:, el * D:(el + 1) * D],
                                         in0=out_acc[:, el * D:(el + 1) * D],
                                         in1=xadd[:, (gi % 2) * D:(gi % 2 + 1) * D]
                                         ).then_inc(s["oa"], 1)
                    q.memset(pfd[0:1, 0:1], 0.0).then_inc(s["pf"], len(grp))

        @block.scalar
        def _(q):
            q.wait_ge(s["bar2"], 5)
            for h in range(2):
                for i in range(8):
                    gi = h * 8 + i
                    par = gi % 2
                    q.wait_ge(s["oa"], gi + 1)
                    if gi >= 2:
                        q.wait_ge(s["od%d" % par], 16 * (gi // 2))
                    q.dma_start(out=outT[gi * 128:(gi + 1) * 128, :],
                                in_=out_acc[:, i * D:(i + 1) * D]
                                ).then_inc(s["od%d" % par], 16)
            q.wait_ge(s["od0"], 16 * 8)
            q.wait_ge(s["od1"], 16 * 8)

        p2.close()

    es.close()
    return nc


def kernel(L0_attn_x, L0_attn_W_in, L0_attn_W_out, L0_attn_log_A,
           L0_attn_W_dt, L0_attn_b_dt, L0_ffn_w1, L0_ffn_w2, L0_ffn_w3):
    x = np.ascontiguousarray(np.asarray(L0_attn_x, dtype=np.float32))
    W_in = np.ascontiguousarray(np.asarray(L0_attn_W_in, dtype=np.float32))
    W_out = np.asarray(L0_attn_W_out, dtype=np.float32)
    log_A = np.asarray(L0_attn_log_A, dtype=np.float32)
    W_dt = np.asarray(L0_attn_W_dt, dtype=np.float32)
    b_dt = np.asarray(L0_attn_b_dt, dtype=np.float32)
    w1 = np.asarray(L0_ffn_w1, dtype=np.float32)
    w2 = np.asarray(L0_ffn_w2, dtype=np.float32)
    w3 = np.asarray(L0_ffn_w3, dtype=np.float32)

    winT = np.ascontiguousarray(W_in.T)
    wdtT = np.ascontiguousarray(W_dt.T)
    woutT = np.ascontiguousarray(W_out.T)
    bdt = np.ascontiguousarray(b_dt.reshape(H, 1))
    lasel = np.full((H, D), -1e4, np.float32)
    for h in range(H):
        lasel[h, h * 64:(h + 1) * 64] = log_A[h, :]
    w1t = np.ascontiguousarray(
        w1.T.reshape(ND, 128, NF, 128).transpose(2, 1, 0, 3).reshape(NF, 128, D))
    w2t = np.ascontiguousarray(
        w2.T.reshape(ND, 128, NF, 128).transpose(2, 1, 0, 3).reshape(NF, 128, D))
    w3t = np.ascontiguousarray(w3.T)

    gflat = np.stack([w1.reshape(-1), w2.reshape(-1), w3.reshape(-1)])
    PSL = 128 * GSLICE

    nc = build_nc()
    in_maps = []
    for c in range(8):
        bb, t0 = c // 4, (c % 4) * T
        xt_b = x[bb].T
        xt_core = np.zeros((D, TT), np.float32)
        lo = t0 - WARM
        if lo < 0:
            xt_core[:, WARM:] = xt_b[:, t0:t0 + T]
        else:
            xt_core[:, :] = xt_b[:, lo:t0 + T]
        gsh_c = np.ascontiguousarray(
            gflat[:, c * PSL:(c + 1) * PSL].reshape(3, 128, GSLICE))
        in_maps.append({
            "xt_in": np.ascontiguousarray(xt_core),
            "win": W_in, "winT": winT, "wdtT": wdtT, "bdt": bdt,
            "lasel": lasel, "woutT": woutT,
            "w1t": w1t, "w2t": w2t, "w3t": w3t, "gsh": gsh_c,
        })
    res = run_bass_kernel_spmd(nc, in_maps, core_ids=list(range(8)))
    out = np.empty((2, 8192, D), np.float32)
    for c in range(8):
        bb, t0 = c // 4, (c % 4) * T
        out[bb, t0:t0 + T, :] = res.results[c]["outT"].T
    return out


# version 16
# speedup vs baseline: 1.0369x; 1.0281x over previous
import numpy as np
from contextlib import ExitStack

import concourse.bass as bass
import concourse.mybir as mybir
from concourse.bass_utils import run_bass_kernel_spmd

F32 = mybir.dt.float32
BF16 = mybir.dt.bfloat16
AF = mybir.ActivationFunctionType
ALU = mybir.AluOpType
AX = mybir.AxisListType.X

D = 2048
ND = 16
F = 5632
NF = 44
H = 32
T = 2048
WARM = 256
TT = T + WARM
TS = 512
BLOCKS = [(0, 256), (256, 512), (768, 512), (1280, 512), (1792, 512)]
NB = len(BLOCKS)
NWELEM = float(F * D)
NPAIR = NF + ND
PGROUP = 10
RING = 21
GSLICE = F * D // 8 // 128
GW = 512
NGJOB = GSLICE // GW
NGAMMA = 3 * NGJOB
GS = 2

EPS_LN = 1e-5
EPS_Q = 1e-8


def build_nc():
    nc = bass.Bass("TRN2", target_bir_lowering=False, debug=False)

    xt_in = nc.dram_tensor("xt_in", [D, TT], F32, kind="ExternalInput").ap()
    win = nc.dram_tensor("win", [D, D], F32, kind="ExternalInput").ap()
    winT = nc.dram_tensor("winT", [D, D], F32, kind="ExternalInput").ap()
    wdtT = nc.dram_tensor("wdtT", [D, H], F32, kind="ExternalInput").ap()
    bdt_in = nc.dram_tensor("bdt", [H, 1], F32, kind="ExternalInput").ap()
    lasel = nc.dram_tensor("lasel", [H, D], F32, kind="ExternalInput").ap()
    woutT = nc.dram_tensor("woutT", [D, D], F32, kind="ExternalInput").ap()
    w1t = nc.dram_tensor("w1t", [NF, 128, D], F32, kind="ExternalInput").ap()
    w2t = nc.dram_tensor("w2t", [NF, 128, D], F32, kind="ExternalInput").ap()
    w3t = nc.dram_tensor("w3t", [F, D], F32, kind="ExternalInput").ap()
    gsh = nc.dram_tensor("gsh", [3, 128, GSLICE], F32, kind="ExternalInput").ap()
    outT = nc.dram_tensor("outT", [D, T], F32, kind="ExternalOutput").ap()
    gci_t = nc.dram_tensor("gci", [1, 16], F32)
    gco_t = nc.dram_tensor("gco", [1, 16], F32)
    gci = gci_t.ap()
    gco = gco_t.ap()

    yssp = nc.dram_tensor("yssp", [D, T], BF16, kind="Internal").ap()
    woutbf = nc.dram_tensor("woutbf", [D, D], BF16, kind="Internal").ap()
    hidsp = nc.dram_tensor("hidsp", [F, T], BF16, kind="Internal").ap()
    q3sp = nc.dram_tensor("q3sp", [F, T], BF16, kind="Internal").ap()
    taur = nc.dram_tensor("taur", [1, 8], F32, kind="Internal").ap()

    es = ExitStack()

    def sb(name, shape, dtype):
        return es.enter_context(nc.sbuf_tensor(name, shape, dtype))

    def ps(name, shape):
        return es.enter_context(nc.psum_tensor(name, shape, F32))

    def sem(name):
        return es.enter_context(nc.semaphore(name))

    taub = sb("taub", [128, 8], F32)
    xn_es = ExitStack()
    xn_all = xn_es.enter_context(nc.sbuf_tensor("xn_all", [128, ND * TT], BF16))

    s = {}
    for n in (["cl", "ll", "winl0", "winl1", "wdl", "wtl", "wcm", "ae", "wce",
               "xl", "sq", "sqf", "smm", "f1", "f2", "f3", "f4",
               "ib", "xn", "dmm", "z1", "z2", "z3", "Lm",
               "xpm", "u", "a", "scf", "sc", "ysp0", "ysp1",
               "gl0", "gl1", "gl2", "gcA", "gcV", "gfz", "gmm", "gz",
               "gpw", "gpd", "ccs", "grd",
               "tw", "tb", "bar1",
               "wl0", "wl1", "q1", "q2", "q3", "x1m", "x2m",
               "sg", "hd", "hsp0", "hsp1", "q3sp0", "q3sp1", "bar2",
               "pf", "p2m", "p2f", "xa0", "xa1", "oa", "od",
               "wbl0", "wbl1", "wbs0", "wbs1", "od0", "od1"]
              + ["pl%d" % i for i in range(RING)]):
        s[n] = sem("s_" + n)

    gjobs = [(i // NGJOB, i % NGJOB) for i in range(NGAMMA)]
    gbound = [list(range(17 * k, min(17 * k + 17, NGAMMA))) for k in range(6)]
    NG_A = sum(1 for i in range(NGAMMA) if i % 2 == 0)
    NG_V = NGAMMA - NG_A

    def gcnt(j):
        a = sum(1 for i in range(j + 1) if i % 2 == 0)
        return a, (j + 1) - a

    cum_ysp_par = [[], []]
    np_ = [0, 0]
    for b in range(NB):
        for e in range(ND):
            if b >= 1:
                np_[e % 2] += 1
            cum_ysp_par[0].append(np_[0])
            cum_ysp_par[1].append(np_[1])

    pairs = [("f", f) for f in range(NF)] + [("d", d) for d in range(ND)]
    groups = [list(range(i, min(i + PGROUP, NPAIR))) for i in range(0, NPAIR, PGROUP)]

    with nc.Block() as block:
        a_es = ExitStack()

        def asb(name, shape, dtype):
            return a_es.enter_context(nc.sbuf_tensor(name, shape, dtype))

        def aps(name, shape):
            return a_es.enter_context(nc.psum_tensor(name, shape, F32))

        winT_bf = asb("winT_bf", [128, ND * D], BF16)
        xT = asb("xT", [128, 2 * ND * TS], BF16)

        def xTb(b):
            p_ = b % 2
            return xT[:, p_ * ND * TS:(p_ + 1) * ND * TS]
        gstr = asb("gstr", [128, GS * GW], F32)
        wdtT_sb = asb("wdtT_sb", [128, ND * H], BF16)
        ASel = asb("ASel", [H, D], BF16)
        WcT_bf = asb("WcT_bf", [128, ND * H], BF16)
        bdt_sb = asb("bdt_sb", [H, 1], F32)
        ones_bf = asb("ones_bf", [128, 1], BF16)
        ones_f = asb("ones_f", [128, 1], F32)
        ones_r = asb("ones_r", [1, 128], F32)
        state = asb("state", [128, ND], F32)
        u_sl = asb("u_sl", [128, 2 * TS], F32)
        a_sl = asb("a_sl", [128, 2 * TS], F32)
        ys_sl = asb("ys_sl", [128, 2 * TS], F32)
        xsq_sl = asb("xsq_sl", [128, 4 * TS], BF16)
        zw = asb("zw", [H, 2 * TS], BF16)
        srow = asb("srow", [1, 5 * TS], F32)
        gacc = asb("gacc", [128, NGAMMA + 4], F32)
        gcol = asb("gcol", [128, 4], F32)
        tau_sb = asb("tau_sb", [1, 8], F32)
        gpart_sb = asb("gpart_sb", [1, 16], F32)
        gred_sb = asb("gred_sb", [1, 16], F32)

        pS = aps("pS", [128, TS])
        pQ = aps("pQ", [128, TS])
        pdt = aps("pdt", [128, TS])
        pL = [aps("pL0", [128, TS]), aps("pL1", [128, TS])]
        pxp = [aps("pxp0", [128, TS]), aps("pxp1", [128, TS])]
        pwc = aps("pwc", [128, TS])
        pSl = [pS, pwc]
        pQl = [pQ, pdt]

        win_slot = [xTb(1)[:, 0:D], xTb(1)[:, D:2 * D]]

        @block.sync
        def _(q):
            gload_i = 0

            def gamma_loads(k):
                nonlocal gload_i
                for i in gbound[k]:
                    m, c = gjobs[i]
                    if i >= GS:
                        q.wait_ge(s["gcA"], i - GS + 1)
                    slot = gstr[:, (i % GS) * GW:(i % GS + 1) * GW]
                    src = gsh[m, :, c * GW:(c + 1) * GW]
                    q.dma_start(out=slot, in_=src).then_inc(s["gl%d" % (i % GS)], 16)
                    gload_i += 1

            q.dma_start(out=bdt_sb[:, :], in_=bdt_in[:, :]).then_inc(s["cl"], 16)
            gamma_loads(0)
            for b in range(NB):
                gamma_loads(b + 1)
            q.wait_ge(s["gcA"], NGAMMA)
            q.nop().then_inc(s["bar1"], 1)

        @block.tensor
        def _(q):
            q.wait_ge(s["wdl"], 16 * ND)
            for e in range(ND):
                q.wait_ge(s["winl%d" % (e % 2)], 16 * (e // 2 + 1))
                last = None
                for d in range(ND):
                    last = q.matmul(pwc[:, d * H:(d + 1) * H],
                                    win_slot[e % 2][:, d * 128:(d + 1) * 128],
                                    wdtT_sb[:, e * H:(e + 1) * H],
                                    start=(e == 0 and d == 0),
                                    stop=(e == ND - 1 and d == ND - 1),
                                    skip_group_check=True)
                last.then_inc(s["wcm"], 1)
            def pe_stats(b, jlist):
                pS_, pQ_ = pSl[b % 2], pQl[b % 2]
                Tb_ = BLOCKS[b][1]
                for j in jlist:
                    idx = 16 * b + j
                    if j == 0:
                        if b == 0:
                            q.wait_ge(s["gz"], 1)
                        if b >= 1:
                            q.wait_ge(s["f3"], b)
                            q.wait_ge(s["xn"], 16 * b)
                        if b % 2 == 1:
                            q.wait_ge(s["wce"], 1)
                            q.wait_ge(s["z1"], b)
                        q.wait_ge(s["xl"], 16 * 16 * (b + 1))
                    q.matmul(pS_[0:1, 0:Tb_], ones_bf[:, 0:1],
                             xTb(b)[:, j * TS:j * TS + Tb_],
                             start=(j == 0), stop=(j == ND - 1),
                             skip_group_check=True)
                    q.wait_ge(s["sq"], idx + 1)
                    q.matmul(pQ_[0:1, 0:Tb_], ones_bf[:, 0:1],
                             xsq_sl[:, (idx % 4) * TS:(idx % 4) * TS + Tb_],
                             start=(j == 0), stop=(j == ND - 1),
                             skip_group_check=True).then_inc(s["sqf"], 1)


            def pe_ib(b):
                pS_, pQ_ = pSl[b % 2], pQl[b % 2]
                Tb_ = BLOCKS[b][1]
                q.wait_ge(s["f3"], b + 1)
                q.matmul(pS_[:, 0:Tb_], ones_r[0:1, :],
                         srow[0:1, 3 * TS:3 * TS + Tb_],
                         start=True, stop=True)
                q.wait_ge(s["f4"], b + 1)
                q.matmul(pQ_[:, 0:Tb_], ones_r[0:1, :],
                         srow[0:1, 4 * TS:4 * TS + Tb_],
                         start=True, stop=True).then_inc(s["ib"], 1)

            def pe_dt(b):
                c0_, Tb_ = BLOCKS[b]
                q.wait_ge(s["xn"], 16 * (b + 1))
                q.wait_ge(s["wce"], 1)
                if b >= 1:
                    q.wait_ge(s["z1"], b)
                last = None
                for d in range(ND):
                    last = q.matmul(pdt[0:H, 0:Tb_],
                                    WcT_bf[:, d * H:(d + 1) * H],
                                    xn_all[:, d * TT + c0_:d * TT + c0_ + Tb_],
                                    start=(d == 0), stop=(d == ND - 1))
                last.then_inc(s["dmm"], 1)

            pe_stats(0, range(ND))
            pe_ib(0)
            pe_dt(0)
            for b, (c0, Tb) in enumerate(BLOCKS):
                for e in range(ND):
                    le = 16 * b + e
                    if e == 0:
                        q.wait_ge(s["z3"], b + 1)
                    if le >= 2:
                        q.wait_ge(s["a"], le - 1)
                    q.matmul(pL[e % 2][:, 0:Tb],
                             ASel[:, e * 128:(e + 1) * 128],
                             zw[:, 0:Tb],
                             start=True, stop=True).then_inc(s["Lm"], 1)
                    if le == 0:
                        q.wait_ge(s["wtl"], 16 * ND)
                    if le >= 2:
                        q.wait_ge(s["u"], le - 1)
                    last = None
                    for d in range(ND):
                        last = q.matmul(pxp[e % 2][:, 0:Tb],
                                        winT_bf[:, d * D + e * 128:d * D + (e + 1) * 128],
                                        xn_all[:, d * TT + c0:d * TT + c0 + Tb],
                                        start=(d == 0), stop=(d == ND - 1))
                    last.then_inc(s["xpm"], 1)
                    if b == 3 and e == 0:
                        q.wait_ge(s["xn"], 16 * 4)
                        for m in range(3):
                            q.wait_ge(s["gfz"], m + 1)
                            q.matmul(pwc[0:1, m:m + 1], gcol[:, m:m + 1],
                                     ones_f[:, 0:1],
                                     start=(m == 0), stop=(m == 2),
                                     skip_group_check=True).then_inc(s["gmm"], 1)
                    if b + 1 < NB:
                        if 1 <= e <= 8:
                            pe_stats(b + 1, range(2 * (e - 1), 2 * e))
                        elif e == 11:
                            pe_ib(b + 1)
                if b + 1 < NB:
                    pe_dt(b + 1)
            q.nop().then_inc(s["bar1"], 1)

        @block.scalar
        def _(q):
            gc_done = 0

            def gamma_jobs(k):
                if k == 0:
                    q.wait_ge(s["gz"], 1)
                for i in gbound[k]:
                    sl_ = gstr[:, (i % GS) * GW:(i % GS + 1) * GW]
                    q.wait_ge(s["gl%d" % (i % GS)], 16 * (i // GS + 1))
                    q.activation(out=sl_, in_=sl_, func=AF.Abs,
                                 accum_out=gacc[:, i:i + 1]).then_inc(s["gcA"], 1)

            q.wait_ge(s["ll"], 16)
            q.activation(out=ASel[:, :], in_=ASel[:, :], func=AF.Exp).then_inc(s["ae"], 1)
            gamma_jobs(0)

            def act_squares(b, jlist):
                Tb_ = BLOCKS[b][1]
                for j in jlist:
                    idx = 16 * b + j
                    if j == 0:
                        q.wait_ge(s["xl"], 16 * 16 * (b + 1))
                    if idx >= 4:
                        q.wait_ge(s["sqf"], idx - 3)
                    q.activation(out=xsq_sl[:, (idx % 4) * TS:(idx % 4) * TS + Tb_],
                                 in_=xTb(b)[:, j * TS:j * TS + Tb_],
                                 func=AF.Square).then_inc(s["sq"], 1)

            def act_f1(b):
                pS_, pQ_ = pSl[b % 2], pQl[b % 2]
                Tb_ = BLOCKS[b][1]
                q.wait_ge(s["sqf"], 16 * (b + 1))
                q.activation(out=srow[0:1, 0:Tb_], in_=pS_[0:1, 0:Tb_],
                             func=AF.Copy, scale=1.0 / D)
                q.activation(out=srow[0:1, TS:TS + Tb_], in_=pQ_[0:1, 0:Tb_],
                             func=AF.Copy, scale=1.0 / D)
                q.activation(out=srow[0:1, 2 * TS:2 * TS + Tb_], in_=srow[0:1, 0:Tb_],
                             func=AF.Square).then_inc(s["f1"], 1)

            def act_f3(b):
                Tb_ = BLOCKS[b][1]
                q.wait_ge(s["f2"], b + 1)
                q.activation(out=srow[0:1, 4 * TS:4 * TS + Tb_],
                             in_=srow[0:1, 3 * TS:3 * TS + Tb_],
                             func=AF.Ln)
                q.activation(out=srow[0:1, 3 * TS:3 * TS + Tb_],
                             in_=srow[0:1, 4 * TS:4 * TS + Tb_],
                             func=AF.Exp, scale=-0.5).then_inc(s["f3"], 1)

            act_squares(0, range(ND))
            act_f1(0)
            act_f3(0)
            for b, (c0, Tb) in enumerate(BLOCKS):
                q.wait_ge(s["z1"], b + 1)
                q.activation(out=zw[:, TS:TS + Tb], in_=zw[:, 0:Tb], func=AF.Abs)
                q.activation(out=zw[:, TS:TS + Tb], in_=zw[:, TS:TS + Tb],
                             func=AF.Exp, scale=-1.0)
                q.activation(out=zw[:, TS:TS + Tb], in_=zw[:, TS:TS + Tb],
                             func=AF.Ln, bias=1.0).then_inc(s["z2"], 1)
                for e in range(ND):
                    le = 16 * b + e
                    q.wait_ge(s["xpm"], le + 1)
                    if le >= 2:
                        q.wait_ge(s["scf"], le - 1)
                    q.activation(out=u_sl[:, (le % 2) * TS:(le % 2) * TS + Tb],
                                 in_=pxp[e % 2][:, 0:Tb],
                                 func=AF.Copy).then_inc(s["u"], 1)
                    q.wait_ge(s["Lm"], le + 1)
                    q.activation(out=a_sl[:, (le % 2) * TS:(le % 2) * TS + Tb],
                                 in_=pL[e % 2][:, 0:Tb],
                                 func=AF.Exp).then_inc(s["a"], 1)
                    if b + 1 < NB:
                        if e <= 7:
                            act_squares(b + 1, range(2 * e, 2 * e + 2))
                        elif e == 9:
                            act_f1(b + 1)
                        elif e == 10:
                            act_f3(b + 1)
                gamma_jobs(b + 1)
            q.wait_ge(s["gfz"], 4)
            q.dma_start(out=taur[:, :], in_=tau_sb[0:1, :]).then_inc(s["tw"], 16)
            q.wait_ge(s["tw"], 16)
            for col in [0, 1, 2, 4, 5, 6]:
                q.dma_start(out=taub[:, col:col + 1],
                            in_=taur[0:1, col:col + 1].partition_broadcast(128)
                            ).then_inc(s["tb"], 16)
            q.wait_ge(s["tb"], 16 * 6)
            q.nop().then_inc(s["bar1"], 1)

        @block.vector
        def _(q):
            def gamma_jobs(k):
                if k == 0:
                    q.wait_ge(s["gz"], 1)

            q.memset(state[:, :], 0.0)
            q.memset(gpart_sb[:, :], 0.0)
            q.memset(tau_sb[:, :], 0.0)
            q.memset(ones_bf[:, :], 1.0)
            q.memset(ones_f[:, :], 1.0)
            q.memset(ones_r[:, :], 1.0)
            q.memset(gacc[:, :], 0.0).then_inc(s["gz"], 1)
            q.wait_ge(s["ae"], 1)
            q.tensor_scalar_mul(out=ASel[:, :], in0=ASel[:, :], scalar1=-1.0)
            gamma_jobs(0)

            def dve_f2(b):
                Tb_ = BLOCKS[b][1]
                q.wait_ge(s["f1"], b + 1)
                q.scalar_tensor_tensor(out=srow[0:1, 3 * TS:3 * TS + Tb_],
                                       in0=srow[0:1, TS:TS + Tb_],
                                       scalar=EPS_LN,
                                       in1=srow[0:1, 2 * TS:2 * TS + Tb_],
                                       op0=ALU.add,
                                       op1=ALU.subtract).then_inc(s["f2"], 1)

            def dve_f4(b):
                Tb_ = BLOCKS[b][1]
                q.wait_ge(s["f3"], b + 1)
                q.tensor_mul(out=srow[0:1, 4 * TS:4 * TS + Tb_],
                             in0=srow[0:1, 0:Tb_],
                             in1=srow[0:1, 3 * TS:3 * TS + Tb_]).then_inc(s["f4"], 1)

            def dve_xn(b, jlist):
                pS_, pQ_ = pSl[b % 2], pQl[b % 2]
                c0_, Tb_ = BLOCKS[b]
                for j in jlist:
                    if j == 0:
                        q.wait_ge(s["ib"], b + 1)
                    dst = xn_all[:, j * TT + c0_:j * TT + c0_ + Tb_]
                    q.tensor_mul(out=dst, in0=xTb(b)[:, j * TS:j * TS + Tb_],
                                 in1=pS_[:, 0:Tb_])
                    q.tensor_sub(out=dst, in0=dst,
                                 in1=pQ_[:, 0:Tb_]).then_inc(s["xn"], 1)

            def dve_z(b):
                Tb_ = BLOCKS[b][1]
                q.wait_ge(s["dmm"], b + 1)
                if b == 0:
                    q.wait_ge(s["cl"], 16)
                q.tensor_scalar_add(out=zw[:, 0:Tb_], in0=pdt[0:H, 0:Tb_],
                                    scalar1=bdt_sb[:, 0:1]).then_inc(s["z1"], 1)

            def dve_dtv(b):
                Tb_ = BLOCKS[b][1]
                q.wait_ge(s["z2"], b + 1)
                q.scalar_tensor_tensor(out=zw[:, 0:Tb_], in0=zw[:, 0:Tb_],
                                       scalar=0.0, in1=zw[:, TS:TS + Tb_],
                                       op0=ALU.max,
                                       op1=ALU.add).then_inc(s["z3"], 1)

            XN_SPREAD = {12: range(0, 4), 13: range(4, 8),
                         14: range(8, 12), 15: range(12, 16)}

            dve_f2(0)
            dve_f4(0)
            dve_xn(0, range(ND))
            q.wait_ge(s["wcm"], ND)
            q.tensor_copy(out=WcT_bf[:, :], in_=pwc[:, 0:ND * H]).then_inc(s["wce"], 1)
            dve_z(0)
            dve_dtv(0)
            for b, (c0, Tb) in enumerate(BLOCKS):
                for e in range(ND):
                    le = 16 * b + e
                    q.wait_ge(s["u"], le + 1)
                    q.wait_ge(s["a"], le + 1)
                    if le >= 2 and cum_ysp_par[e % 2][le - 2] > 0:
                        q.wait_ge(s["ysp%d" % (e % 2)],
                                  16 * cum_ysp_par[e % 2][le - 2])
                    q.tensor_tensor_scan(
                        out=ys_sl[:, (le % 2) * TS:(le % 2) * TS + Tb],
                        data0=a_sl[:, (le % 2) * TS:(le % 2) * TS + Tb],
                        data1=u_sl[:, (le % 2) * TS:(le % 2) * TS + Tb],
                        initial=state[:, e:e + 1],
                        op0=ALU.mult, op1=ALU.add).then_inc(s["sc"], 1)
                    q.wait_ge(s["sc"], le + 1)
                    q.tensor_copy(out=state[:, e:e + 1],
                                  in_=ys_sl[:, (le % 2) * TS + Tb - 1:(le % 2) * TS + Tb]
                                  ).then_inc(s["scf"], 1)
                    if b + 1 < NB:
                        if e == 9:
                            dve_f2(b + 1)
                        elif e == 10:
                            dve_f4(b + 1)
                        elif e >= 12:
                            dve_xn(b + 1, XN_SPREAD[e])
                if b + 1 < NB:
                    dve_z(b + 1)
                    dve_dtv(b + 1)
                gamma_jobs(b + 1)
                if b == 2:
                    q.wait_ge(s["gcA"], NGAMMA)
                    for m in range(3):
                        q.tensor_reduce(out=gcol[:, m:m + 1],
                                        in_=gacc[:, m * NGJOB:(m + 1) * NGJOB],
                                        axis=AX, op=ALU.add).then_inc(s["gfz"], 1)
                if b == 3:
                    q.wait_ge(s["gmm"], 3)
                    q.tensor_copy(out=gpart_sb[0:1, 0:3],
                                  in_=pwc[0:1, 0:3]).then_inc(s["gpw"], 1)
            q.wait_ge(s["grd"], 16)
            for m in range(3):
                q.tensor_scalar(out=tau_sb[0:1, m:m + 1], in0=gred_sb[0:1, m:m + 1],
                                scalar1=0.5 / NWELEM, scalar2=0.5 * EPS_Q,
                                op0=ALU.mult, op1=ALU.add)
                q.tensor_scalar(out=tau_sb[0:1, 4 + m:5 + m], in0=gred_sb[0:1, m:m + 1],
                                scalar1=-0.5 / NWELEM, scalar2=-0.5 * EPS_Q,
                                op0=ALU.mult, op1=ALU.add)
            q.tensor_copy(out=tau_sb[0:1, 3:4],
                          in_=tau_sb[0:1, 0:1]).then_inc(s["gfz"], 1)
            q.nop().then_inc(s["bar1"], 1)

        @block.gpsimd
        def _(q):
            q.dma_start(out=ASel[:, :], in_=lasel[:, :]).then_inc(s["ll"], 16)
            for e in range(ND):
                q.dma_start(out=wdtT_sb[:, e * H:(e + 1) * H],
                            in_=wdtT[e * 128:(e + 1) * 128, :]).then_inc(s["wdl"], 16)
            for e in range(ND):
                q.dma_start(out=xTb(0)[:, e * TS:e * TS + BLOCKS[0][1]],
                            in_=xt_in[e * 128:(e + 1) * 128, 0:BLOCKS[0][1]]
                            ).then_inc(s["xl"], 16)
            for e in range(ND):
                if e >= 2:
                    q.wait_ge(s["wcm"], e - 1)
                q.dma_start(out=win_slot[e % 2],
                            in_=win[e * 128:(e + 1) * 128, :]
                            ).then_inc(s["winl%d" % (e % 2)], 16)
            for d in range(ND):
                q.dma_start(out=winT_bf[:, d * D:(d + 1) * D],
                            in_=winT[d * 128:(d + 1) * 128, :]).then_inc(s["wtl"], 16)
            def spill(sb_, e_):
                c0_, Tb_ = BLOCKS[sb_]
                le = 16 * sb_ + e_
                q.wait_ge(s["sc"], le + 1)
                q.dma_start(out=yssp[e_ * 128:(e_ + 1) * 128,
                                     c0_ - WARM:c0_ - WARM + Tb_],
                            in_=ys_sl[:, (le % 2) * TS:(le % 2) * TS + Tb_]
                            ).then_inc(s["ysp%d" % (e_ % 2)], 16)

            for b, (c0, Tb) in enumerate(BLOCKS):
                if b == 0:
                    continue
                for j in range(ND):
                    if b == 1 and j == 0:
                        q.wait_ge(s["wcm"], ND)
                    if b >= 2 and j == 0:
                        q.wait_ge(s["sq"], 16 * b)
                    if b >= 2:
                        q.wait_ge(s["xn"], 16 * (b - 2) + j + 1)
                    q.dma_start(out=xTb(b)[:, j * TS:j * TS + Tb],
                                in_=xt_in[j * 128:(j + 1) * 128, c0:c0 + Tb]
                                ).then_inc(s["xl"], 16)
                if b >= 2:
                    for e in range(ND):
                        spill(b - 1, e)
            q.wait_ge(s["gpw"], 1)
            q.dma_start(out=gci[:, :], in_=gpart_sb[0:1, :]).then_inc(s["gpd"], 16)
            q.wait_ge(s["gpd"], 16)
            q.collective_compute("AllReduce", ALU.add,
                                 replica_groups=[list(range(8))],
                                 ins=[gci_t.ap().opt()],
                                 outs=[gco_t.ap().opt()]).then_inc(s["ccs"])
            q.wait_ge(s["ccs"], 1)
            q.dma_start(out=gred_sb[0:1, :], in_=gco[:, :]).then_inc(s["grd"], 16)
            for e in range(ND):
                spill(NB - 1, e)
            q.wait_ge(s["xl"], 16 * 16 * NB)
            q.wait_ge(s["wtl"], 16 * ND)
            q.wait_ge(s["winl0"], 16 * (ND // 2))
            q.wait_ge(s["winl1"], 16 * (ND // 2))
            q.wait_ge(s["wdl"], 16 * ND)
            q.wait_ge(s["ysp0"], 16 * 32)
            q.wait_ge(s["ysp1"], 16 * 32)
            q.nop().then_inc(s["bar1"], 1)

        a_es.close()

        p1 = ExitStack()

        def p1sb(name, shape, dtype):
            return p1.enter_context(nc.sbuf_tensor(name, shape, dtype))

        ws1 = p1sb("ws1", [128, 2 * D], F32)
        ws2 = p1sb("ws2", [128, 2 * D], F32)
        ws3 = p1sb("ws3", [128, 2 * D], F32)
        q1s = p1sb("q1s", [128, 2 * D], BF16)
        q2s = p1sb("q2s", [128, 2 * D], BF16)
        q3s = p1sb("q3s", [128, 2 * D], BF16)
        m2b = p1sb("m2b", [128, D], F32)
        wb_st = p1sb("wb_st", [128, 2 * D], BF16)
        sig = p1sb("sig", [128, D], F32)
        hid = p1sb("hid", [128, 2 * D], BF16)

        px1 = [p1.enter_context(nc.psum_tensor("px1_%d" % i, [128, TS], F32)) for i in range(4)]
        px2 = [p1.enter_context(nc.psum_tensor("px2_%d" % i, [128, TS], F32)) for i in range(4)]

        @block.sync
        def _(q):
            q.wait_ge(s["bar1"], 5)
            for f in range(NF):
                if f >= 2:
                    q.wait_ge(s["q1"], f - 1)
                    q.wait_ge(s["q2"], f - 1)
                    q.wait_ge(s["q3"], f - 1)
                sl = slice((f % 2) * D, (f % 2 + 1) * D)
                wsem = s["wl%d" % (f % 2)]
                q.dma_start(out=ws1[:, sl], in_=w1t[f, :, :]).then_inc(wsem, 16)
                q.dma_start(out=ws2[:, sl], in_=w2t[f, :, :]).then_inc(wsem, 16)
                q.dma_start(out=ws3[:, sl], in_=w3t[f * 128:(f + 1) * 128, :]).then_inc(wsem, 16)
            q.wait_ge(s["wl0"], 16 * 3 * (NF // 2))
            q.wait_ge(s["wl1"], 16 * 3 * (NF // 2))
            q.nop().then_inc(s["bar2"], 1)

        @block.gpsimd
        def _(q):
            def wout_pair(d):
                par = d % 2
                wsl = slice(par * D, (par + 1) * D)
                if d >= 2:
                    q.wait_ge(s["wbs%d" % par], 16 * (d // 2))
                q.dma_start(out=wb_st[:, wsl],
                            in_=woutT[d * 128:(d + 1) * 128, :]
                            ).then_inc(s["wbl%d" % par], 16)
                q.wait_ge(s["wbl%d" % par], 16 * (d // 2 + 1))
                q.dma_start(out=woutbf[d * 128:(d + 1) * 128, :],
                            in_=wb_st[:, wsl]).then_inc(s["wbs%d" % par], 16)

            q.wait_ge(s["bar1"], 5)
            for f in range(NF):
                sl = slice((f % 2) * D, (f % 2 + 1) * D)
                q.wait_ge(s["q3"], f + 1)
                q.dma_start(out=q3sp[f * 128:(f + 1) * 128, :],
                            in_=q3s[:, sl]).then_inc(s["q3sp%d" % (f % 2)], 16)
                q.wait_ge(s["hd"], 4 * f + 4)
                q.dma_start(out=hidsp[f * 128:(f + 1) * 128, :],
                            in_=hid[:, sl]).then_inc(s["hsp%d" % (f % 2)], 16)
                if f >= 2 and f % 2 == 0 and (f - 2) // 2 < ND:
                    wout_pair((f - 2) // 2)
            q.wait_ge(s["q3sp0"], 16 * (NF // 2))
            q.wait_ge(s["q3sp1"], 16 * (NF // 2))
            q.wait_ge(s["hsp0"], 16 * (NF // 2))
            q.wait_ge(s["hsp1"], 16 * (NF // 2))
            q.wait_ge(s["wbs0"], 16 * (ND // 2))
            q.wait_ge(s["wbs1"], 16 * (ND // 2))
            q.nop().then_inc(s["bar2"], 1)

        @block.vector
        def _(q):
            def quant(g):
                sl = slice((g % 2) * D, (g % 2 + 1) * D)
                q.wait_ge(s["wl%d" % (g % 2)], 16 * 3 * (g // 2 + 1))
                if g >= 2:
                    q.wait_ge(s["x2m"], g - 1)
                for (wsb, qsb_, tcol, qsem) in ((ws1, q1s, 0, "q1"),
                                                (ws2, q2s, 1, "q2"),
                                                (ws3, q3s, 2, "q3")):
                    if qsem == "q3" and g >= 2:
                        q.wait_ge(s["q3sp%d" % (g % 2)], 16 * (g // 2))
                    q.tensor_single_scalar(out=m2b[:, :], in_=wsb[:, sl],
                                           scalar=taub[:, 4 + tcol:5 + tcol],
                                           op=ALU.is_lt)
                    q.scalar_tensor_tensor(out=qsb_[:, sl], in0=wsb[:, sl],
                                           scalar=taub[:, tcol:tcol + 1],
                                           in1=m2b[:, :], op0=ALU.is_gt,
                                           op1=ALU.subtract).then_inc(s[qsem], 1)

            q.wait_ge(s["bar1"], 5)
            quant(0)
            quant(1)
            for f in range(NF):
                if f >= 2:
                    q.wait_ge(s["hsp%d" % (f % 2)], 16 * (f // 2))
                q.wait_ge(s["x2m"], f + 1)
                for tb_ in range(4):
                    q.wait_ge(s["sg"], 4 * f + tb_ + 1)
                    q.tensor_mul(out=hid[:, (f % 2) * D + tb_ * TS:(f % 2) * D + (tb_ + 1) * TS],
                                 in0=sig[:, tb_ * TS:(tb_ + 1) * TS],
                                 in1=px2[tb_][:, :]).then_inc(s["hd"], 1)
                if f + 2 < NF:
                    quant(f + 2)
            q.nop().then_inc(s["bar2"], 1)

        @block.scalar
        def _(q):
            q.wait_ge(s["bar1"], 5)
            for f in range(NF):
                q.wait_ge(s["x1m"], f + 1)
                for tb_ in range(4):
                    if f >= 1:
                        q.wait_ge(s["hd"], 4 * (f - 1) + tb_ + 1)
                    q.activation(out=sig[:, tb_ * TS:(tb_ + 1) * TS],
                                 in_=px1[tb_][:, :],
                                 func=AF.Sigmoid).then_inc(s["sg"], 1)
            q.wait_ge(s["hd"], 4 * NF)
            q.nop().then_inc(s["bar2"], 1)

        @block.tensor
        def _(q):
            q.wait_ge(s["bar1"], 5)
            for f in range(NF):
                sl0 = (f % 2) * D
                q.wait_ge(s["q1"], f + 1)
                if f >= 1:
                    q.wait_ge(s["sg"], 4 * f)
                last = None
                for d in range(ND):
                    for tb_ in range(4):
                        last = q.matmul(px1[tb_][:, :],
                                        q1s[:, sl0 + d * 128:sl0 + (d + 1) * 128],
                                        xn_all[:, d * TT + WARM + tb_ * TS:d * TT + WARM + (tb_ + 1) * TS],
                                        start=(d == 0), stop=(d == ND - 1),
                                        skip_group_check=True)
                last.then_inc(s["x1m"], 1)
                q.wait_ge(s["q2"], f + 1)
                if f >= 1:
                    q.wait_ge(s["hd"], 4 * f)
                last = None
                for d in range(ND):
                    for tb_ in range(4):
                        last = q.matmul(px2[tb_][:, :],
                                        q2s[:, sl0 + d * 128:sl0 + (d + 1) * 128],
                                        xn_all[:, d * TT + WARM + tb_ * TS:d * TT + WARM + (tb_ + 1) * TS],
                                        start=(d == 0), stop=(d == ND - 1),
                                        skip_group_check=True)
                last.then_inc(s["x2m"], 1)
            q.nop().then_inc(s["bar2"], 1)

        p1.close()
        xn_es.close()

        p2 = ExitStack()

        def p2sb(name, shape, dtype):
            return p2.enter_context(nc.sbuf_tensor(name, shape, dtype))

        Lring = p2sb("Lring", [128, RING * 1024], BF16)
        Rring = p2sb("Rring", [128, RING * D], BF16)
        out_acc = p2sb("out_acc", [128, 8 * D], F32)
        xadd = p2sb("xadd", [128, 2 * D], F32)
        pfd = p2sb("pfd", [1, 4], F32)

        pp2 = [p2.enter_context(nc.psum_tensor("pp2_%d" % i, [128, TS], F32)) for i in range(2)]

        @block.sync
        def _(q):
            q.wait_ge(s["bar2"], 5)
            for h in range(2):
                for p, (kind, idx) in enumerate(pairs):
                    gp = h * NPAIR + p
                    if gp >= RING:
                        q.wait_ge(s["pf"], gp - RING + 1)
                    slot = gp % RING
                    psem = s["pl%d" % slot]
                    if kind == "f":
                        q.dma_start(out=Rring[:, slot * D:(slot + 1) * D],
                                    in_=hidsp[idx * 128:(idx + 1) * 128, :]
                                    ).then_inc(psem, 16)
                        q.dma_start(out=Lring[:, slot * 1024:(slot + 1) * 1024],
                                    in_=q3sp[idx * 128:(idx + 1) * 128,
                                             h * 1024:(h + 1) * 1024]
                                    ).then_inc(psem, 16)
                    else:
                        q.dma_start(out=Rring[:, slot * D:(slot + 1) * D],
                                    in_=yssp[idx * 128:(idx + 1) * 128, :]
                                    ).then_inc(psem, 16)
                        q.dma_start(out=Lring[:, slot * 1024:(slot + 1) * 1024],
                                    in_=woutbf[idx * 128:(idx + 1) * 128,
                                               h * 1024:(h + 1) * 1024]
                                    ).then_inc(psem, 16)
                for i in range(8):
                    gi = h * 8 + i
                    if gi >= 2:
                        q.wait_ge(s["od%d" % ((gi - 1) % 2)],
                                  16 * ((gi - 1) // 2 + 1))
                    q.dma_start(out=xadd[:, (gi % 2) * D:(gi % 2 + 1) * D],
                                in_=xt_in[gi * 128:(gi + 1) * 128, WARM:TT]
                                ).then_inc(s["xa%d" % (gi % 2)], 16)

        @block.tensor
        def _(q):
            q.wait_ge(s["bar2"], 5)
            tile = 0
            for h in range(2):
                for g, grp in enumerate(groups):
                    for p in grp:
                        gp = h * NPAIR + p
                        q.wait_ge(s["pl%d" % (gp % RING)],
                                  16 * 2 * (gp // RING + 1))
                    for el in range(8):
                        for tb_ in range(4):
                            if tile >= 2:
                                q.wait_ge(s["p2f"], tile - 1)
                            bank = pp2[tile % 2]
                            last = None
                            for i, p in enumerate(grp):
                                slot = (h * NPAIR + p) % RING
                                last = q.matmul(bank[:, :],
                                                Lring[:, slot * 1024 + el * 128:slot * 1024 + (el + 1) * 128],
                                                Rring[:, slot * D + tb_ * TS:slot * D + (tb_ + 1) * TS],
                                                start=(i == 0), stop=(i == len(grp) - 1))
                            last.then_inc(s["p2m"], 1)
                            tile += 1

        @block.vector
        def _(q):
            q.wait_ge(s["bar2"], 5)
            tile = 0
            for h in range(2):
                for g, grp in enumerate(groups):
                    last_g = (g == len(groups) - 1)
                    for el in range(8):
                        if h == 1 and g == 0:
                            q.wait_ge(s["od%d" % (el % 2)], 16 * (el // 2 + 1))
                        for tb_ in range(4):
                            q.wait_ge(s["p2m"], tile + 1)
                            dst = out_acc[:, el * D + tb_ * TS:el * D + (tb_ + 1) * TS]
                            if g == 0:
                                q.tensor_copy(out=dst, in_=pp2[tile % 2][:, :]).then_inc(s["p2f"], 1)
                            else:
                                q.tensor_add(out=dst, in0=dst,
                                             in1=pp2[tile % 2][:, :]).then_inc(s["p2f"], 1)
                            tile += 1
                        if last_g:
                            gi = h * 8 + el
                            q.wait_ge(s["xa%d" % (gi % 2)], 16 * (gi // 2 + 1))
                            q.tensor_add(out=out_acc[:, el * D:(el + 1) * D],
                                         in0=out_acc[:, el * D:(el + 1) * D],
                                         in1=xadd[:, (gi % 2) * D:(gi % 2 + 1) * D]
                                         ).then_inc(s["oa"], 1)
                    q.memset(pfd[0:1, 0:1], 0.0).then_inc(s["pf"], len(grp))

        @block.scalar
        def _(q):
            q.wait_ge(s["bar2"], 5)
            for h in range(2):
                for i in range(8):
                    gi = h * 8 + i
                    par = gi % 2
                    q.wait_ge(s["oa"], gi + 1)
                    if gi >= 2:
                        q.wait_ge(s["od%d" % par], 16 * (gi // 2))
                    q.dma_start(out=outT[gi * 128:(gi + 1) * 128, :],
                                in_=out_acc[:, i * D:(i + 1) * D]
                                ).then_inc(s["od%d" % par], 16)
            q.wait_ge(s["od0"], 16 * 8)
            q.wait_ge(s["od1"], 16 * 8)

        p2.close()

    es.close()
    return nc


def kernel(L0_attn_x, L0_attn_W_in, L0_attn_W_out, L0_attn_log_A,
           L0_attn_W_dt, L0_attn_b_dt, L0_ffn_w1, L0_ffn_w2, L0_ffn_w3):
    x = np.ascontiguousarray(np.asarray(L0_attn_x, dtype=np.float32))
    W_in = np.ascontiguousarray(np.asarray(L0_attn_W_in, dtype=np.float32))
    W_out = np.asarray(L0_attn_W_out, dtype=np.float32)
    log_A = np.asarray(L0_attn_log_A, dtype=np.float32)
    W_dt = np.asarray(L0_attn_W_dt, dtype=np.float32)
    b_dt = np.asarray(L0_attn_b_dt, dtype=np.float32)
    w1 = np.asarray(L0_ffn_w1, dtype=np.float32)
    w2 = np.asarray(L0_ffn_w2, dtype=np.float32)
    w3 = np.asarray(L0_ffn_w3, dtype=np.float32)

    winT = np.ascontiguousarray(W_in.T)
    wdtT = np.ascontiguousarray(W_dt.T)
    woutT = np.ascontiguousarray(W_out.T)
    bdt = np.ascontiguousarray(b_dt.reshape(H, 1))
    lasel = np.full((H, D), -1e4, np.float32)
    for h in range(H):
        lasel[h, h * 64:(h + 1) * 64] = log_A[h, :]
    w1t = np.ascontiguousarray(
        w1.T.reshape(ND, 128, NF, 128).transpose(2, 1, 0, 3).reshape(NF, 128, D))
    w2t = np.ascontiguousarray(
        w2.T.reshape(ND, 128, NF, 128).transpose(2, 1, 0, 3).reshape(NF, 128, D))
    w3t = np.ascontiguousarray(w3.T)

    gflat = np.stack([w1.reshape(-1), w2.reshape(-1), w3.reshape(-1)])
    PSL = 128 * GSLICE

    nc = build_nc()
    in_maps = []
    for c in range(8):
        bb, t0 = c // 4, (c % 4) * T
        xt_b = x[bb].T
        xt_core = np.zeros((D, TT), np.float32)
        lo = t0 - WARM
        if lo < 0:
            xt_core[:, WARM:] = xt_b[:, t0:t0 + T]
        else:
            xt_core[:, :] = xt_b[:, lo:t0 + T]
        gsh_c = np.ascontiguousarray(
            gflat[:, c * PSL:(c + 1) * PSL].reshape(3, 128, GSLICE))
        in_maps.append({
            "xt_in": np.ascontiguousarray(xt_core),
            "win": W_in, "winT": winT, "wdtT": wdtT, "bdt": bdt,
            "lasel": lasel, "woutT": woutT,
            "w1t": w1t, "w2t": w2t, "w3t": w3t, "gsh": gsh_c,
        })
    res = run_bass_kernel_spmd(nc, in_maps, core_ids=list(range(8)))
    out = np.empty((2, 8192, D), np.float32)
    for c in range(8):
        bb, t0 = c // 4, (c % 4) * T
        out[bb, t0:t0 + T, :] = res.results[c]["outT"].T
    return out
